# version 03
# baseline (speedup 1.0000x reference)
import numpy as np
from contextlib import ExitStack

import concourse.bass as bass
import concourse.mybir as mybir
from concourse.bass_utils import run_bass_kernel_spmd

F32 = mybir.dt.float32
BF16 = mybir.dt.bfloat16
U8 = mybir.dt.uint8
AF = mybir.ActivationFunctionType
ALU = mybir.AluOpType
AX = mybir.AxisListType

ENGS = ("sync", "tensor", "scalar", "vector", "gpsimd")
PROG_ENGS = ("tensor", "scalar", "vector", "gpsimd")

D = 1024
C = 2048
NF = D // 128
NC_ = C // 128
NJ = 16
NH = 16
HD = 64
NE = 8
EPS = 1e-5


class Chan:
    __slots__ = ("sem", "n", "name")

    def __init__(self, sem, name):
        self.sem = sem
        self.n = 0
        self.name = name


TILE_NAMES = {}


class Tile:
    __slots__ = ("h", "name", "w", "rd")

    def __init__(self, h, name):
        self.h = h
        self.name = name
        self.w = None
        self.rd = {}
        try:
            TILE_NAMES[h.name] = name
        except Exception:
            pass


class Em:
    def __init__(self, nc, es):
        self.nc = nc
        self.es = es
        self.q = {e: [] for e in ENGS}
        self.idx = {e: 0 for e in ENGS}
        self.need = {e: set() for e in ENGS}
        self.prog = {e: es.enter_context(nc.semaphore("prog_" + e))
                     for e in PROG_ENGS}
        self.chans = []
        self.pending = {}

    def chan(self, name):
        if not hasattr(self, "_chan_cache"):
            self._chan_cache = {}
        if name in self._chan_cache:
            return self._chan_cache[name]
        ch = Chan(self.es.enter_context(self.nc.semaphore("ch_" + name)), name)
        self.chans.append(ch)
        self._chan_cache[name] = ch
        return ch

    def _rdep(self, t):
        k, a, v = t.w
        if k == "d":
            return ("d", a, a.n)
        return t.w

    def emit(self, eng, build, reads=(), writes=(), dma=None):
        deps = []
        for t in reads:
            if t.w is not None:
                deps.append(self._rdep(t))
        for t in writes:
            if t.w is not None:
                deps.append(self._rdep(t))
            deps.extend(t.rd.values())
        if self.pending.get(eng):
            deps.extend(self.pending[eng])
            self.pending[eng] = None
        best = {}
        for d in deps:
            if d[0] == "e":
                key = ("e", d[1])
            else:
                key = ("d", id(d[1]))
            if key not in best or d[2] > best[key][2]:
                best[key] = d
        fdeps = list(best.values())
        for d in fdeps:
            if d[0] == "e":
                self.need[d[1]].add(d[2])
        if dma is not None:
            dma.n += 16
            tag = ("d", dma, dma.n)
            self.q[eng].append((fdeps, build, dma, None))
        else:
            self.idx[eng] += 1
            tag = ("e", eng, self.idx[eng])
            self.q[eng].append((fdeps, build, None, self.idx[eng]))
        for t in writes:
            t.w = tag
            t.rd = {}
        rkey = ("d", id(dma)) if dma is not None else ("e", eng)
        for t in reads:
            t.rd[rkey] = tag
        return tag

    def barrier(self):
        deps = [("e", e, self.idx[e]) for e in PROG_ENGS if self.idx[e] > 0]
        deps += [("d", ch, ch.n) for ch in self.chans if ch.n > 0]
        for e in ENGS:
            self.pending[e] = list(deps)
        for d in deps:
            if d[0] == "e":
                self.need[d[1]].add(d[2])

    def finalize(self):
        rank = {e: {i: r + 1 for r, i in enumerate(sorted(self.need[e]))}
                for e in PROG_ENGS}

        def runner(eng):
            def run(e):
                known = {}
                for fdeps, build, dma, myidx in self.q[eng]:
                    for d in fdeps:
                        if d[0] == "e":
                            tgt = rank[d[1]][d[2]]
                            sem = self.prog[d[1]]
                            key = ("e", d[1])
                        else:
                            tgt = d[2]
                            sem = d[1].sem
                            key = ("d", id(d[1]))
                        if known.get(key, 0) < tgt:
                            e.wait_ge(sem, tgt)
                            known[key] = tgt
                    if build is None:
                        continue
                    ins = build(e)
                    if dma is not None:
                        ins.then_inc(dma.sem, 16)
                    elif myidx in self.need[eng]:
                        ins.then_inc(self.prog[eng], 1)
            return run

        with self.nc.Block() as block:
            for eng in ENGS:
                getattr(block, eng)(runner(eng))


class SB:
    def __init__(self, nc, total):
        self.nc = nc
        self.h = nc.alloc_sbuf_tensor("arena", [128, total], U8)
        self.base = nc.lookup_mloc(self.h).addr
        self.total = total
        self.off = 0
        self.ph_base = None
        self.ph_off = 0
        self.cnt = 0

    def _mk(self, shape, dtype, off):
        nbytes = int(np.prod(shape[1:])) * mybir.dt.size(dtype)
        assert off + nbytes <= self.total, \
            ("sbuf overflow", off, nbytes, self.total)
        self.cnt += 1
        return self.nc.alloc_sbuf_tensor_at(
            "sb%d" % self.cnt, list(shape), dtype, offset=self.base + off)

    def persist(self, shape, dtype=F32):
        assert self.ph_base is None
        off = (self.off + 31) & ~31
        h = self._mk(shape, dtype, off)
        self.off = off + int(np.prod(shape[1:])) * mybir.dt.size(dtype)
        return h

    def phase_begin(self):
        if self.ph_base is None:
            self.ph_base = (self.off + 127) & ~127
        self.ph_off = self.ph_base

    def mark(self):
        return self.ph_off

    def reset(self, mk):
        self.ph_off = mk

    def alloc(self, shape, dtype=F32):
        off = (self.ph_off + 31) & ~31
        h = self._mk(shape, dtype, off)
        self.ph_off = off + int(np.prod(shape[1:])) * mybir.dt.size(dtype)
        self.last_rng = (off, self.ph_off)
        return h


def build_program(S):
    NQ = S // 512
    SQ = 512
    SH = S // 2
    NKT = S // 128

    nc = bass.Bass("TRN2", target_bir_lowering=False, debug=False)
    es = ExitStack()
    em = Em(nc, es)

    def dram(name, shape, kind="Internal"):
        return nc.dram_tensor(name, list(shape), F32, kind=kind)

    x_in = dram("x", (S, D), "ExternalInput")
    out = dram("out", (S, D), "ExternalOutput")
    w_in = dram("w_in", (D, 2 * C), "ExternalInput")
    w_x = dram("w_x", (C, 96), "ExternalInput")
    w_dt = dram("w_dt", (64, C), "ExternalInput")
    w_out = dram("w_out", (C, D), "ExternalInput")
    wq_r = dram("wq_r", (NH * 128, NF * HD), "ExternalInput")
    wk_r = dram("wk_r", (NH * 128, NF * HD), "ExternalInput")
    wv_d = dram("wv", (D, D), "ExternalInput")
    wo = dram("wo", (D, D), "ExternalInput")
    w1_r = dram("w1_r", (NE * 16 * 128, 8 * 128), "ExternalInput")
    w2_r = dram("w2_r", (NE * 16 * 128, 8 * 128), "ExternalInput")
    w3_2d = dram("w3", (NE * C, D), "ExternalInput")
    vecs = dram("vecs", (128, 48), "ExternalInput")
    vecd_d = dram("vecd", (128, 40), "ExternalInput")
    cw_d = dram("cw_r", (128, 64), "ExternalInput")
    alog_d = dram("alog_r", (128, 16 * NC_), "ExternalInput")
    gw_d = dram("gw_r", (128, 64), "ExternalInput")
    ident_d = dram("ident", (128, 128), "ExternalInput")
    masks_d = dram("masks", (128, 4 * 512), "ExternalInput")
    h_s = dram("h_s", (C, S))
    gate_s = dram("gate_s", (C, S))
    x0_s = dram("x0_s", (D, S))
    x1_s = dram("x1_s", (D, S))
    g_s = dram("g_s", (D, S))
    a_s = dram("a_s", (D, S))
    xln2_s = dram("xln2_s", (D, S))
    ctx_s = dram("ctx_s", (D, S))
    bc_s = dram("bc_s", (32, S))
    wte_s = dram("wte_s", (8, S))
    dramb = lambda name, shape: nc.dram_tensor(name, list(shape), BF16,
                                               kind="Internal")
    w1h_s = dramb("w1h_s", (NE * 16 * 128, 8 * 128))
    w1l_s = dramb("w1l_s", (NE * 16 * 128, 8 * 128))
    w2h_s = dramb("w2h_s", (NE * 16 * 128, 8 * 128))
    w2l_s = dramb("w2l_s", (NE * 16 * 128, 8 * 128))
    w3h_s = dramb("w3h_s", (NE * C, D))
    w3l_s = dramb("w3l_s", (NE * C, D))
    xl2h_s = dramb("xl2h_s", (D, S))
    xl2l_s = dramb("xl2l_s", (D, S))

    rem = nc.sbuf_bytes_remaining
    sb = SB(nc, rem - 4096)

    T = Tile
    ident = T(sb.persist([128, 128]), "ident")
    ones = T(sb.persist([128, 128]), "ones")
    cw = T(sb.persist([128, 64]), "cw")
    vec = T(sb.persist([128, 48]), "vec")
    vecd = T(sb.persist([128, 40]), "vecd")
    apb = T(sb.persist([128, 16 * NC_]), "apb")
    gw = T(sb.persist([128, 64]), "gw")
    gwh = T(sb.persist([128, 64], BF16), "gwh")
    gwl = T(sb.persist([128, 64], BF16), "gwl")
    ssm = T(sb.persist([96, S]), "ssm")
    carry = T(sb.persist([128, 16 * NC_]), "carry")
    wdt = T(sb.persist([64, C]), "wdt")
    wx = T(sb.persist([128, 96 * NC_]), "wx")
    halo = T(sb.persist([128, 3 * NC_]), "halo")
    rows = T(sb.persist([1, 3072]), "rows")
    sqr_t = [T(sb.persist([128, 512]), "lnsq%d" % i) for i in range(2)]
    lnrb = T(sb.persist([128, 512]), "lnrb")
    lnmb = T(sb.persist([128, 512]), "lnmb")

    PS = [T(nc.alloc_psum_tensor("ps%d" % i, [128, 512], F32), "ps%d" % i)
          for i in range(8)]

    alias_live = []

    def AT(shape, name, dtype=F32):
        h = sb.alloc(shape, dtype)
        lo, hi = sb.last_rng
        t = Tile(h, name)
        for (a, b, pt) in alias_live:
            if a < hi and lo < b and pt is not t:
                if pt.w is not None:
                    t.rd[("al", id(pt), "w")] = pt.w
                for k, v in pt.rd.items():
                    t.rd[("al", id(pt), k)] = v
        alias_live.append((lo, hi, t))
        return t

    def t_mm(o_ap, l_ap, r_ap, st, sp, rd, wr):
        em.emit("tensor",
                lambda e, o=o_ap, l=l_ap, r=r_ap, a=st, b=sp:
                e.matmul(o, l, r, start=a, stop=b), rd, wr)

    def t_tr(o_ap, i_ap, rd, wr, idw=128):
        em.emit("tensor",
                lambda e, o=o_ap, i=i_ap, d=ident.h[0:idw, 0:idw]:
                e.transpose(o, i, d), list(rd) + [ident], wr)

    def mm3(ps_ap, lh, ll, rh, rl, start, stop, rd, wr):
        t_mm(ps_ap, lh, rh, start, False, rd, wr)
        t_mm(ps_ap, lh, rl, False, False, rd, wr)
        t_mm(ps_ap, ll, rh, False, stop, rd, wr)

    def s_act(o_ap, i_ap, fn, rd, wr, bias=0.0, scale=1.0):
        em.emit("scalar",
                lambda e, o=o_ap, i=i_ap, f=fn, b=bias, s=scale:
                e.activation(o, i, f, bias=b, scale=s), rd, wr)

    def s_cp(o_ap, i_ap, rd, wr):
        em.emit("scalar", lambda e, o=o_ap, i=i_ap: e.copy(o, i), rd, wr)

    def v_tt(op, o_ap, a_ap, b_ap, rd, wr, eng="vector"):
        em.emit(eng, lambda e, o=o_ap, a=a_ap, b=b_ap, p=op:
                e.tensor_tensor(o, a, b, op=p), rd, wr)

    def v_ts(o_ap, i_ap, s1, s2, op0, op1, rd, wr, eng="vector"):
        em.emit(eng, lambda e, o=o_ap, i=i_ap, a=s1, b=s2, p0=op0, p1=op1:
                e.tensor_scalar(o, i, a, b, op0=p0, op1=p1), rd, wr)

    def v_stt(o_ap, i0_ap, sc, i1_ap, op0, op1, rd, wr, eng="vector"):
        em.emit(eng, lambda e, o=o_ap, i0=i0_ap, s=sc, i1=i1_ap, p0=op0, p1=op1:
                e.scalar_tensor_tensor(o, i0, s, i1, op0=p0, op1=p1), rd, wr)

    def v_scan(o_ap, d0_ap, d1_ap, init, rd, wr):
        em.emit("vector", lambda e, o=o_ap, a=d0_ap, b=d1_ap, i=init:
                e.tensor_tensor_scan(o, a, b, i, op0=ALU.mult, op1=ALU.add),
                rd, wr)

    def v_cp(o_ap, i_ap, rd, wr, eng="vector"):
        em.emit(eng, lambda e, o=o_ap, i=i_ap: e.tensor_copy(o, i), rd, wr)

    def v_max(o_ap, i_ap, rd, wr):
        em.emit("vector", lambda e, o=o_ap, i=i_ap:
                e.reduce_max(o, i, axis=AX.X), rd, wr)

    def v_rec(o_ap, i_ap, rd, wr):
        em.emit("vector", lambda e, o=o_ap, i=i_ap: e.reciprocal(o, i), rd, wr)

    def v_ms(ap, val, t, eng="vector"):
        em.emit(eng, lambda e, o=ap, v=val: e.memset(o, v), [], [t])

    def dma(ch, o_ap, i_ap, rd, wr, eng="sync"):
        em.emit(eng, lambda e, o=o_ap, i=i_ap: e.dma_start(out=o, in_=i),
                rd, wr, dma=ch)

    def ln_fm(srcs, dst_fn, W, scale=None, shift=None, par_t=None, res=None,
              psA=None, psB=None):
        psA = psA or PS[6]
        psB = psB or PS[7]
        nk = len(srcs)
        for k, (t, ap) in enumerate(srcs):
            t_mm(psA.h[0:1, 0:W], ones.h[:, 0:1], ap, k == 0, k == nk - 1,
                 [t, ones], [psA])
        for k, (t, ap) in enumerate(srcs):
            sq = sqr_t[k % 2]
            s_act(sq.h[:, 0:W], ap, AF.Square, [t], [sq])
            t_mm(psB.h[0:1, 0:W], ones.h[:, 0:1], sq.h[:, 0:W], k == 0,
                 k == nk - 1, [sq, ones], [psB])
        mu = rows.h[0:1, 0:W]
        m2 = rows.h[0:1, 512:512 + W]
        veps = rows.h[0:1, 1024:1024 + W]
        rstd = rows.h[0:1, 1536:1536 + W]
        mur = rows.h[0:1, 2048:2048 + W]
        tmp = rows.h[0:1, 2560:2560 + W]
        inv = 1.0 / D
        v_ts(mu, psA.h[0:1, 0:W], inv, None, ALU.mult, ALU.bypass, [psA], [rows])
        v_ts(m2, psB.h[0:1, 0:W], inv, None, ALU.mult, ALU.bypass, [psB], [rows])
        v_tt(ALU.mult, tmp, mu, mu, [rows], [rows])
        v_tt(ALU.subtract, tmp, m2, tmp, [rows], [rows])
        v_ts(veps, tmp, EPS, None, ALU.add, ALU.bypass, [rows], [rows])
        s_act(rstd, veps, AF.Sqrt, [rows], [rows])
        v_rec(rstd, rstd, [rows], [rows])
        v_tt(ALU.mult, tmp, rstd, rstd, [rows], [rows])
        v_tt(ALU.mult, tmp, tmp, veps, [rows], [rows])
        v_ts(tmp, tmp, -0.5, 1.5, ALU.mult, ALU.add, [rows], [rows])
        v_tt(ALU.mult, rstd, rstd, tmp, [rows], [rows])
        v_tt(ALU.mult, mur, mu, rstd, [rows], [rows])
        t_mm(psA.h[:, 0:W], ones.h[0:1, :], rstd, True, True, [rows, ones], [psA])
        t_mm(psB.h[:, 0:W], ones.h[0:1, :], mur, True, True, [rows, ones], [psB])
        s_cp(lnrb.h[:, 0:W], psA.h[:, 0:W], [psA], [lnrb])
        s_cp(lnmb.h[:, 0:W], psB.h[:, 0:W], [psB], [lnmb])
        for k, (t, ap) in enumerate(srcs):
            dt_, dap = dst_fn(k)
            v_tt(ALU.mult, dap, ap, lnrb.h[:, 0:W], [t, lnrb], [dt_])
            v_tt(ALU.subtract, dap, dap, lnmb.h[:, 0:W], [dt_, lnmb], [dt_])
            if scale is not None:
                v_ts(dap, dap, scale(k), shift(k), ALU.mult, ALU.add,
                     [dt_, par_t], [dt_])
            if res is not None:
                rt, rap = res(k)
                v_tt(ALU.add, dap, dap, rap, [dt_, rt], [dt_])

    ch_const = em.chan("const")
    dma(ch_const, ident.h[:, :], ident_d[:, :], [], [ident])
    dma(ch_const, cw.h[:, :], cw_d[:, :], [], [cw])
    dma(ch_const, vec.h[:, :], vecs[:, :], [], [vec])
    dma(ch_const, vecd.h[:, :], vecd_d[:, :], [], [vecd])
    dma(ch_const, apb.h[:, :], alog_d[:, :], [], [apb])
    dma(ch_const, gw.h[:, :], gw_d[:, :], [], [gw])
    dma(ch_const, wdt.h[:, :], w_dt[:, :], [], [wdt])
    for c in range(NC_):
        dma(ch_const, wx.h[:, 96 * c:96 * (c + 1)],
            w_x[128 * c:128 * (c + 1), :], [], [wx])
    v_ms(ones.h[:, :], 1.0, ones)
    s_cp(gwh.h[:, :], gw.h[:, :], [gw], [gwh])
    v_tt(ALU.subtract, gwl.h[:, :], gw.h[:, :], gwh.h[:, :], [gw, gwh], [gwl])
    s_act(apb.h[:, :], apb.h[:, :], AF.Exp, [apb], [apb])
    v_ts(apb.h[:, :], apb.h[:, :], -1.0, None, ALU.mult, ALU.bypass,
         [apb], [apb])
    for c in range(NC_):
        v_ms(halo.h[:, 3 * c:3 * (c + 1)], 0.0, halo)

    CB = lambda c: vec.h[:, c:c + 1]
    BDT = lambda c: vec.h[:, 16 + c:17 + c]
    DV = lambda c: vec.h[:, 32 + c:33 + c]
    LN1S = lambda k: vecd.h[:, k:k + 1]
    LN1B = lambda k: vecd.h[:, 8 + k:9 + k]
    LN2S = lambda k: vecd.h[:, 16 + k:17 + k]
    LN2B = lambda k: vecd.h[:, 24 + k:25 + k]
    BO = lambda k: vecd.h[:, 32 + k:33 + k]

    sb.phase_begin()
    x_tm = [AT([128, D], "x_tm%d" % i) for i in range(2)]
    ch_xtm = [em.chan("xtm%d" % i) for i in range(2)]
    x0q = [AT([128, SQ], "x0q%d" % k) for k in range(NF)]
    xlnq = [AT([128, SQ], "xlnq%d" % k) for k in range(NF)]
    winh = [AT([128, C], "winh%d" % k) for k in range(NF)]
    ch_win = [em.chan("win")] * NF
    rawx = [AT([128, 515], "rawx%d" % i) for i in range(2)]
    hbuf = [AT([128, 512], "hbuf%d" % i) for i in range(2)]
    gbuf = [AT([128, 512], "gbuf%d" % i) for i in range(2)]
    ch_sp_h = em.chan("sp_h")
    ch_sp_g = em.chan("sp_g")
    ch_sp_x0 = em.chan("sp_x0")
    hs_t = T(h_s, "h_s")
    gs_t = T(gate_s, "gate_s")
    x0s_t = T(x0_s, "x0_s")

    NSQ = S // SQ
    NT = SQ // 512
    for q in range(NSQ):
        for scn in range(SQ // 128):
            xt = x_tm[scn % 2]
            s0 = q * SQ + scn * 128
            dma(ch_xtm[scn % 2], xt.h[:, :], x_in[s0:s0 + 128, :], [], [xt])
            for k in range(NF):
                ps = PS[k % 2]
                t_tr(ps.h[:, 0:128], xt.h[:, 128 * k:128 * (k + 1)], [xt], [ps])
                s_cp(x0q[k].h[:, 128 * scn:128 * (scn + 1)], ps.h[:, 0:128],
                     [ps], [x0q[k]])
        for t in range(NT):
            w0 = 512 * t
            ln_fm([(x0q[k], x0q[k].h[:, w0:w0 + 512]) for k in range(NF)],
                  lambda k, w0=w0: (xlnq[k], xlnq[k].h[:, w0:w0 + 512]), 512)
        for k in range(NF):
            dma(ch_sp_x0, x0_s[128 * k:128 * (k + 1), q * SQ:(q + 1) * SQ],
                x0q[k].h[:, :], [x0q[k]], [x0s_t])
        for half in range(2):
            for k in range(NF):
                dma(ch_win[k], winh[k].h[:, :],
                    w_in[128 * k:128 * (k + 1), half * C:(half + 1) * C],
                    [], [winh[k]])
            for m in range(NC_):
                for t in range(NT):
                    ps = PS[2 + (m + t) % 2]
                    for k in range(NF):
                        t_mm(ps.h[:, :], winh[k].h[:, 128 * m:128 * (m + 1)],
                             xlnq[k].h[:, 512 * t:512 * (t + 1)],
                             k == 0, k == NF - 1, [winh[k], xlnq[k]], [ps])
                    scol = q * SQ + 512 * t
                    if half == 0:
                        c = m
                        rx = rawx[(m + t) % 2]
                        v_cp(rx.h[:, 0:3], halo.h[:, 3 * c:3 * (c + 1)],
                             [halo], [rx])
                        s_cp(rx.h[:, 3:515], ps.h[:, :], [ps], [rx])
                        v_cp(halo.h[:, 3 * c:3 * (c + 1)], rx.h[:, 512:515],
                             [rx], [halo])
                        hb = hbuf[(m + t) % 2]
                        v_ts(hb.h[:, :], rx.h[:, 0:512],
                             cw.h[:, 4 * c:4 * c + 1], None, ALU.mult,
                             ALU.bypass, [rx, cw], [hb])
                        for kk in range(1, 4):
                            v_stt(hb.h[:, :], rx.h[:, kk:kk + 512],
                                  cw.h[:, 4 * c + kk:4 * c + kk + 1],
                                  hb.h[:, :], ALU.mult, ALU.add,
                                  [rx, cw, hb], [hb])
                        s_act(hb.h[:, :], hb.h[:, :], AF.Silu, [hb, vec], [hb],
                              bias=CB(c))
                        psm = PS[4 + t % 2]
                        t_mm(psm.h[0:96, :], wx.h[:, 96 * c:96 * (c + 1)],
                             hb.h[:, :], c == 0, c == NC_ - 1, [wx, hb], [psm])
                        if c == NC_ - 1:
                            s_cp(ssm.h[:, scol:scol + 512], psm.h[0:96, :],
                                 [psm], [ssm])
                        dma(ch_sp_h, h_s[128 * c:128 * (c + 1), scol:scol + 512],
                            hb.h[:, :], [hb], [hs_t])
                    else:
                        gb = gbuf[(m + t) % 2]
                        s_cp(gb.h[:, :], ps.h[:, :], [ps], [gb])
                        dma(ch_sp_g,
                            gate_s[128 * m:128 * (m + 1), scol:scol + 512],
                            gb.h[:, :], [gb], [gs_t])

    ch_sp_bc = em.chan("sp_bc")
    bcs_t = T(bc_s, "bc_s")
    dma(ch_sp_bc, bc_s[:, :], ssm.h[64:96, :], [ssm], [bcs_t])

    sb.phase_begin()
    NCQ = 4
    dtq = [AT([128, 512], "dtq%d" % i) for i in range(NCQ)]
    dthq = [AT([128, 512], "dthq%d" % i) for i in range(NCQ)]
    hq = [AT([128, 512], "hq%d" % i) for i in range(NCQ)]
    ch_hq = [em.chan("hq%d" % i) for i in range(NCQ)]
    yacc = [AT([128, 512], "yacc%d" % i) for i in range(NCQ)]
    yq = [AT([128, 512], "yq%d" % c) for c in range(NC_)]
    bb = [AT([128, 512], "bb%d" % i) for i in range(2)]
    cbb = [AT([128, 512], "cbb%d" % i) for i in range(2)]
    ch_bb = [em.chan("bbc%d" % i) for i in range(2)]
    ch_cb = [em.chan("cbc%d" % i) for i in range(2)]
    dA = [AT([128, 512], "dA%d" % i) for i in range(4)]
    dBu = [AT([128, 512], "dBu%d" % i) for i in range(4)]
    stl = [AT([128, 512], "st%d" % i) for i in range(4)]
    gq = [AT([128, 512], "gq%d" % i) for i in range(2)]
    sp1 = AT([128, 512], "sp1")
    sp2 = AT([128, 512], "sp2")
    ch_gq = [em.chan("gq%d" % i) for i in range(2)]
    sgq = AT([128, 512], "sgq")
    woutb = [AT([128, D], "woutb%d" % i) for i in range(2)]
    ch_wout = [em.chan("wout%d" % i) for i in range(2)]
    ast = [AT([128, 512], "ast%d" % k) for k in range(NF)]
    x0r = [AT([128, 512], "x0r%d" % i) for i in range(2)]
    ch_x0r = [em.chan("x0r%d" % i) for i in range(2)]
    ch_sp_x1 = em.chan("sp_x1")
    x1s_t = T(x1_s, "x1_s")

    for q in range(NSQ):
        s0 = q * SQ
        for cq in range(NC_ // NCQ):
            cbase = cq * NCQ
            for ci in range(NCQ):
                c = cbase + ci
                ps = PS[ci % 2]
                t_mm(ps.h[:, :], wdt.h[0:64, 128 * c:128 * (c + 1)],
                     ssm.h[0:64, s0:s0 + 512], True, True, [wdt, ssm], [ps])
                s_act(sp1.h[:, :], ps.h[:, :], AF.Abs, [ps, vec], [sp1],
                      bias=BDT(c))
                s_act(sp2.h[:, :], sp1.h[:, :], AF.Exp, [sp1], [sp2],
                      scale=-1.0)
                s_act(sp2.h[:, :], sp2.h[:, :], AF.Ln, [sp2], [sp2], bias=1.0)
                s_act(sp1.h[:, :], ps.h[:, :], AF.Relu, [ps, vec], [sp1],
                      bias=BDT(c))
                v_tt(ALU.add, dtq[ci].h[:, :], sp1.h[:, :], sp2.h[:, :],
                     [sp1, sp2], [dtq[ci]])
                dma(ch_hq[ci], hq[ci].h[:, :],
                    h_s[128 * c:128 * (c + 1), s0:s0 + 512], [hs_t], [hq[ci]])
                v_tt(ALU.mult, dthq[ci].h[:, :], dtq[ci].h[:, :],
                     hq[ci].h[:, :], [dtq[ci], hq[ci]], [dthq[ci]])
            for j in range(NJ):
                bbt = bb[j % 2]
                cbt = cbb[j % 2]
                dma(ch_bb[j % 2], bbt.h[:, :],
                    bc_s[j:j + 1, s0:s0 + 512].partition_broadcast(128),
                    [bcs_t], [bbt])
                dma(ch_cb[j % 2], cbt.h[:, :],
                    bc_s[16 + j:17 + j, s0:s0 + 512].partition_broadcast(128),
                    [bcs_t], [cbt])
                for ci in range(NCQ):
                    c = cbase + ci
                    cj = 16 * c + j
                    da = dA[ci]
                    db = dBu[ci]
                    st_t = stl[ci]
                    s_act(da.h[:, :], dtq[ci].h[:, :], AF.Exp,
                          [dtq[ci], apb], [da],
                          scale=apb.h[:, cj:cj + 1])
                    v_tt(ALU.mult, db.h[:, :], dthq[ci].h[:, :], bbt.h[:, :],
                         [dthq[ci], bbt], [db])
                    init = 0.0 if q == 0 else carry.h[:, cj:cj + 1]
                    rd = [da, db] + ([carry] if q > 0 else [])
                    v_scan(st_t.h[:, :], da.h[:, :], db.h[:, :], init,
                           rd, [st_t])
                    if q < NSQ - 1:
                        v_cp(carry.h[:, cj:cj + 1], st_t.h[:, 511:512],
                             [st_t], [carry])
                    v_tt(ALU.mult, da.h[:, :], st_t.h[:, :], cbt.h[:, :],
                         [st_t, cbt], [da])
                    if j == 0:
                        v_cp(yacc[ci].h[:, :], da.h[:, :], [da], [yacc[ci]])
                    else:
                        v_tt(ALU.add, yacc[ci].h[:, :], yacc[ci].h[:, :],
                             da.h[:, :], [yacc[ci], da], [yacc[ci]])
            for ci in range(NCQ):
                c = cbase + ci
                gt = gq[ci % 2]
                dma(ch_gq[ci % 2], gt.h[:, :],
                    gate_s[128 * c:128 * (c + 1), s0:s0 + 512], [gs_t], [gt])
                s_act(sgq.h[:, :], gt.h[:, :], AF.Silu, [gt], [sgq])
                v_stt(yacc[ci].h[:, :], hq[ci].h[:, :], DV(c),
                      yacc[ci].h[:, :], ALU.mult, ALU.add,
                      [hq[ci], vec, yacc[ci]], [yacc[ci]])
                v_tt(ALU.mult, yq[c].h[:, :], yacc[ci].h[:, :], sgq.h[:, :],
                     [yacc[ci], sgq], [yq[c]])
        for k in range(NC_):
            wb = woutb[k % 2]
            dma(ch_wout[k % 2], wb.h[:, :], w_out[128 * k:128 * (k + 1), :],
                [], [wb])
            for f in range(NF):
                t_mm(PS[f].h[:, :], wb.h[:, 128 * f:128 * (f + 1)],
                     yq[k].h[:, :], k == 0, k == NC_ - 1, [wb, yq[k]], [PS[f]])
        for f in range(NF):
            s_cp(ast[f].h[:, :], PS[f].h[:, :], [PS[f]], [ast[f]])
        ln_fm([(ast[f], ast[f].h[:, :]) for f in range(NF)],
              lambda k: (ast[k], ast[k].h[:, :]), 512)
        for k in range(NF):
            xr = x0r[k % 2]
            dma(ch_x0r[k % 2], xr.h[:, :],
                x0_s[128 * k:128 * (k + 1), s0:s0 + 512], [x0s_t], [xr])
            v_tt(ALU.add, ast[k].h[:, :], ast[k].h[:, :], xr.h[:, :],
                 [ast[k], xr], [ast[k]])
            dma(ch_sp_x1, x1_s[128 * k:128 * (k + 1), s0:s0 + 512],
                ast[k].h[:, :], [ast[k]], [x1s_t])

    sb.phase_begin()
    masks = AT([128, 4 * 512], "masks")
    rows2 = AT([1, 1152], "rows2")
    xa = [AT([128, S], "xa%d" % k) for k in range(NF)]
    _p3mk = sb.mark()
    x1r = [AT([128, 512], "x1r%d" % k) for k in range(NF)]
    ch_x1r = [em.chan("x1r")] * NF
    gb2 = [AT([128, 512], "gb2_%d" % k) for k in range(NF)]
    ch_sp_g2 = em.chan("sp_g2")
    gs2_t = T(g_s, "g_s")
    ch_masks = em.chan("masks")
    dma(ch_masks, masks.h[:, :], masks_d[:, :], [], [masks])

    for t in range(NQ):
        w0 = 512 * t
        for k in range(NF):
            dma(ch_x1r[k], x1r[k].h[:, :],
                x1_s[128 * k:128 * (k + 1), w0:w0 + 512], [x1s_t], [x1r[k]])
        ln_fm([(x1r[k], x1r[k].h[:, :]) for k in range(NF)],
              lambda k: (gb2[k], gb2[k].h[:, :]), 512)
        for k in range(NF):
            dma(ch_sp_g2, g_s[128 * k:128 * (k + 1), w0:w0 + 512],
                gb2[k].h[:, :], [gb2[k]], [gs2_t])
        ln_fm([(gb2[k], gb2[k].h[:, :]) for k in range(NF)],
              lambda k, w0=w0: (xa[k], xa[k].h[:, w0:w0 + 512]), 512,
              scale=LN1S, shift=LN1B, par_t=vecd)

    sb.reset(_p3mk)
    qhr = [AT([65, S], "qh%d" % i) for i in range(2)]
    khr = [AT([65, S], "kh%d" % i) for i in range(2)]
    vh4 = [AT([128, NKT * 65], "vh%d" % i) for i in range(4)]
    wvb = [AT([128, 256], "wvb%d" % k) for k in range(NF)]
    ch_wvb = [em.chan("wvb")] * NF
    wqs = [AT([128, NF * HD], "wqs%d" % i) for i in range(2)]
    wks = [AT([128, NF * HD], "wks%d" % i) for i in range(2)]
    ch_wqs = [em.chan("wqs%d" % i) for i in range(2)]
    ch_wks = [em.chan("wks%d" % i) for i in range(2)]
    pt = [AT([128, 512], "pt%d" % i) for i in range(2)]
    sqh = AT([64, 512], "sqh")
    ctxh = AT([65, S], "ctxh")
    ch_sp_ctx = em.chan("sp_ctx")
    ctxs_t = T(ctx_s, "ctx_s")
    for vt in vh4:
        for kt in range(NKT):
            v_ms(vt.h[:, 65 * kt + 64:65 * kt + 65], 1.0, vt)

    mk_row = rows2.h[0:1, 0:1]
    nrm = rows2.h[0:1, 64:576]
    augr = rows2.h[0:1, 576:1088]
    ch_aug = em.chan("aug")

    ch_spi = [em.chan("spi%d" % i) for i in range(2)]
    ch_sph = em.chan("sph")
    ch_splo = em.chan("splo")
    w1h_t = T(w1h_s, "w1h_s")
    w1l_t = T(w1l_s, "w1l_s")
    w2h_t = T(w2h_s, "w2h_s")
    w2l_t = T(w2l_s, "w2l_s")
    w3h_t = T(w3h_s, "w3h_s")
    w3l_t = T(w3l_s, "w3l_s")
    spl_jobs = [(w1_r, w1h_t, w1l_t, NE * 16 * 128, 1024),
                (w2_r, w2h_t, w2l_t, NE * 16 * 128, 1024),
                (w3_2d, w3h_t, w3l_t, NE * C, 1024)]
    hi_v = [lnrb.h[:, 0:256].bitcast(BF16), lnrb.h[:, 256:512].bitcast(BF16)]
    lo_v = [lnmb.h[:, 0:256].bitcast(BF16), lnmb.h[:, 256:512].bitcast(BF16)]
    si = 0
    for srcd, hit, lot, nrows, ncols in spl_jobs:
        hid = hit.h
        lod = lot.h
        for r0 in range(0, nrows, 128):
            for c0 in range(0, ncols, 512):
                sl = si % 2
                fx = sqr_t[sl]
                dma(ch_spi[sl], fx.h[:, :], srcd[r0:r0 + 128, c0:c0 + 512],
                    [], [fx])
                s_cp(hi_v[sl], fx.h[:, :], [fx], [lnrb])
                v_tt(ALU.subtract, lo_v[sl], fx.h[:, :], hi_v[sl],
                     [fx, lnrb], [lnmb])
                dma(ch_sph, hid[r0:r0 + 128, c0:c0 + 512], hi_v[sl],
                    [lnrb], [hit])
                dma(ch_splo, lod[r0:r0 + 128, c0:c0 + 512], lo_v[sl],
                    [lnmb], [lot])
                si += 1

    for hg in range(4):
        for k in range(NF):
            dma(ch_wvb[k], wvb[k].h[:, :],
                wv_d[128 * k:128 * (k + 1), 256 * hg:256 * (hg + 1)],
                [], [wvb[k]])
        for kt in range(NKT):
            ps = PS[kt % 2]
            for k in range(NF):
                t_mm(ps.h[0:128, 0:256], xa[k].h[:, 128 * kt:128 * (kt + 1)],
                     wvb[k].h[:, :], k == 0, k == NF - 1, [xa[k], wvb[k]], [ps])
            for hl in range(4):
                s_cp(vh4[hl].h[:, 65 * kt:65 * kt + 64],
                     ps.h[0:128, 64 * hl:64 * hl + 64], [ps], [vh4[hl]])
      
        for hl in range(4):
            hd_ = 4 * hg + hl
            vh = vh4[hl]
            qh = qhr[hd_ % 2]
            kh = khr[hd_ % 2]
            wq_t = wqs[hd_ % 2]
            wk_t = wks[hd_ % 2]
            dma(ch_wqs[hd_ % 2], wq_t.h[:, :],
                wq_r[128 * hd_:128 * (hd_ + 1), :], [], [wq_t])
            dma(ch_wks[hd_ % 2], wk_t.h[:, :],
                wk_r[128 * hd_:128 * (hd_ + 1), :], [], [wk_t])
            for dst, wt in ((qh, wq_t), (kh, wk_t)):
                for t in range(NQ):
                    ps = PS[2 + t % 2]
                    for k in range(NF):
                        t_mm(ps.h[0:64, :], wt.h[:, 64 * k:64 * (k + 1)],
                             xa[k].h[:, 512 * t:512 * (t + 1)], k == 0,
                             k == NF - 1, [xa[k], wt], [ps])
                    s_cp(dst.h[0:64, 512 * t:512 * (t + 1)], ps.h[0:64, :],
                         [ps], [dst])
            v_ms(kh.h[64:65, :], 1.0, kh)
            for t in range(NQ):
                ps = PS[4]
                s_act(sqh.h[:, :], kh.h[0:64, 512 * t:512 * (t + 1)],
                      AF.Square, [kh], [sqh])
                t_mm(ps.h[0:1, :], ones.h[0:64, 0:1], sqh.h[:, :], True, True,
                     [sqh, ones], [ps])
                s_act(nrm, ps.h[0:1, :], AF.Sqrt, [ps], [rows2])
                if t == 0:
                    v_max(mk_row, nrm, [rows2], [rows2])
                else:
                    v_max(rows2.h[0:1, 1:2], nrm, [rows2], [rows2])
                    v_tt(ALU.max, mk_row, mk_row, rows2.h[0:1, 1:2],
                         [rows2], [rows2])
            for t in range(NQ):
                ps = PS[4]
                s_act(sqh.h[:, :], qh.h[0:64, 512 * t:512 * (t + 1)],
                      AF.Square, [qh], [sqh])
                t_mm(ps.h[0:1, :], ones.h[0:64, 0:1], sqh.h[:, :], True, True,
                     [sqh, ones], [ps])
                s_act(nrm, ps.h[0:1, :], AF.Sqrt, [ps], [rows2])
                v_ts(augr, nrm, mk_row, -1.0, ALU.mult, ALU.mult,
                     [rows2], [rows2])
                dma(ch_aug, qh.h[64:65, 512 * t:512 * (t + 1)], augr,
                    [rows2], [qh])
            for t in range(NQ):
                psc = PS[5 + hd_ % 2] if False else PS[5]
                nkt = 4 * (t + 1)
                for kt in range(nkt):
                    pss = PS[6 + kt % 2]
                    t_mm(pss.h[:, :], kh.h[0:65, 128 * kt:128 * (kt + 1)],
                         qh.h[0:65, 512 * t:512 * (t + 1)], True, True,
                         [kh, qh], [pss])
                    pb = pt[kt % 2]
                    s_act(pb.h[:, :], pss.h[:, :], AF.Exp, [pss], [pb],
                          scale=0.125)
                    r = kt - 4 * t
                    if r >= 0:
                        v_tt(ALU.mult, pb.h[:, :], pb.h[:, :],
                             masks.h[:, 512 * r:512 * (r + 1)],
                             [pb, masks], [pb])
                    t_mm(psc.h[0:65, :], vh.h[:, 65 * kt:65 * (kt + 1)],
                         pb.h[:, :], kt == 0, kt == nkt - 1, [vh, pb], [psc])
                s_cp(ctxh.h[0:65, 512 * t:512 * (t + 1)], psc.h[0:65, :],
                     [psc], [ctxh])
                v_rec(ctxh.h[64:65, 512 * t:512 * (t + 1)],
                      ctxh.h[64:65, 512 * t:512 * (t + 1)], [ctxh], [ctxh])
                psr = PS[4]
                t_mm(psr.h[0:64, :], ones.h[64:65, 0:64],
                     ctxh.h[64:65, 512 * t:512 * (t + 1)], True, True,
                     [ctxh, ones], [psr])
                v_tt(ALU.mult, ctxh.h[0:64, 512 * t:512 * (t + 1)],
                     ctxh.h[0:64, 512 * t:512 * (t + 1)], psr.h[0:64, :],
                     [ctxh, psr], [ctxh])
            dma(ch_sp_ctx, ctx_s[64 * hd_:64 * (hd_ + 1), :], ctxh.h[0:64, :],
                [ctxh], [ctxs_t])

    sb.phase_begin()
    wob = [AT([128, D], "wob%d" % i) for i in range(2)]
    ch_wob = [em.chan("wob%d" % i) for i in range(2)]
    ctxr = [AT([128, 512], "ctxr%d" % i) for i in range(3)]
    ch_ctxr = [em.chan("ctxr%d" % i) for i in range(3)]
    gr3 = [AT([128, 512], "gr3_%d" % k) for k in range(NF)]
    ch_gr3 = [em.chan("gr3")] * NF
    ab = [AT([128, 512], "ab%d" % k) for k in range(NF)]
    xl2 = [AT([128, 512], "xl2_%d" % k) for k in range(NF)]
    xl2ht = [AT([128, 512], "xl2ht%d" % i, BF16) for i in range(2)]
    xl2lt = [AT([128, 512], "xl2lt%d" % i, BF16) for i in range(2)]
    ch_sp_a = em.chan("sp_a")
    ch_sp_xl2 = em.chan("sp_xl2")
    ch_sp_xl2l = em.chan("sp_xl2l")
    as_t = T(a_s, "a_s")
    xl2hs_t = T(xl2h_s, "xl2h_s")
    xl2ls_t = T(xl2l_s, "xl2l_s")

    for t in range(NQ):
        w0 = 512 * t
        for k in range(NF):
            cr = ctxr[k % 3]
            dma(ch_ctxr[k % 3], cr.h[:, :],
                ctx_s[128 * k:128 * (k + 1), w0:w0 + 512], [ctxs_t], [cr])
            wb = wob[k % 2]
            dma(ch_wob[k % 2], wb.h[:, :], wo[128 * k:128 * (k + 1), :],
                [], [wb])
            for f in range(NF):
                t_mm(PS[f].h[:, :], wb.h[:, 128 * f:128 * (f + 1)],
                     cr.h[:, :], k == 0, k == NF - 1, [wb, cr], [PS[f]])
        for f in range(NF):
            dma(ch_gr3[f], gr3[f].h[:, :],
                g_s[128 * f:128 * (f + 1), w0:w0 + 512], [gs2_t], [gr3[f]])
        for f in range(NF):
            v_stt(ab[f].h[:, :], PS[f].h[:, :], BO(f), gr3[f].h[:, :],
                  ALU.add, ALU.add, [PS[f], vecd, gr3[f]], [ab[f]])
            dma(ch_sp_a, a_s[128 * f:128 * (f + 1), w0:w0 + 512],
                ab[f].h[:, :], [ab[f]], [as_t])
        ln_fm([(ab[f], ab[f].h[:, :]) for f in range(NF)],
              lambda k: (xl2[k], xl2[k].h[:, :]), 512,
              scale=LN2S, shift=LN2B, par_t=vecd)
        for k in range(NF):
            ht = xl2ht[k % 2]
            lt = xl2lt[k % 2]
            s_cp(ht.h[:, :], xl2[k].h[:, :], [xl2[k]], [ht])
            v_tt(ALU.subtract, lt.h[:, :], xl2[k].h[:, :], ht.h[:, :],
                 [xl2[k], ht], [lt])
            dma(ch_sp_xl2, xl2h_s[128 * k:128 * (k + 1), w0:w0 + 512],
                ht.h[:, :], [ht], [xl2hs_t])
            dma(ch_sp_xl2l, xl2l_s[128 * k:128 * (k + 1), w0:w0 + 512],
                lt.h[:, :], [lt], [xl2ls_t])

    sb.phase_begin()
    xhh = [AT([128, SH], "xhh%d" % k, BF16) for k in range(NF)]
    xll = [AT([128, SH], "xll%d" % k, BF16) for k in range(NF)]
    ch_xhh = [em.chan("xlh")] * NF
    ch_xll = [em.chan("xlh")] * NF
    heh = [AT([128, SH], "heh%d" % m, BF16) for m in range(8)]
    hel = [AT([128, SH], "hel%d" % m, BF16) for m in range(8)]
    hf = [AT([128, 512], "hf0")] * 2
    acc = [AT([128, SH], "acc%d" % k) for k in range(NF)]
    w1b = [AT([128, 8 * 128], "w1b%d" % i, BF16) for i in range(4)]
    w2b = [AT([128, 8 * 128], "w2b%d" % i, BF16) for i in range(4)]
    _w12a, _w12b = em.chan("w12a"), em.chan("w12b")
    ch_w1b = [_w12a, _w12a, _w12b, _w12b]
    ch_w2b = [_w12a, _w12a, _w12b, _w12b]
    w3hb = [AT([128, 512], "w3hb%d" % i, BF16) for i in range(3)]
    w3lb = [AT([128, 512], "w3lb%d" % i, BF16) for i in range(3)]
    ch_w3 = [em.chan("w3c%d" % i) for i in range(3)]
    silu_b = AT([128, 512], "silu_b")
    tmpb = AT([128, 512], "tmpb")
    wtm = AT([128, 64], "wtm")
    wte = AT([8, SH], "wte")
    wtes_t = T(wte_s, "wte_s")
    ch_sp_wte = em.chan("sp_wte")
    routeb = AT([128, SH], "routeb")
    ch_rb = em.chan("rb")
    gsc = AT([8, 512], "gsc")
    r8 = AT([128, 8], "r8")
    r8b = AT([128, 8], "r8b")
    rcol = AT([128, 4], "rcol")
    ar2 = [AT([128, 512], "ar2_%d" % k) for k in range(NF)]
    ch_ar2 = [em.chan("ar2")] * NF
    x1r2 = [AT([128, 512], "x1r2_%d" % i) for i in range(2)]
    ch_x1r2 = [em.chan("x1r2%d" % i) for i in range(2)]
    outst = AT([128, 512], "outst")
    ch_out = em.chan("outc")
    out_t = T(out, "out")

    NTH = SH // 512
    for sh in range(2):
        s0 = sh * SH
        for k in range(NF):
            dma(ch_xhh[k], xhh[k].h[:, :],
                xl2h_s[128 * k:128 * (k + 1), s0:s0 + SH], [xl2hs_t], [xhh[k]])
            dma(ch_xll[k], xll[k].h[:, :],
                xl2l_s[128 * k:128 * (k + 1), s0:s0 + SH], [xl2ls_t], [xll[k]])
            v_ms(acc[k].h[:, :], 0.0, acc[k])
        for t in range(NTH):
            ps = PS[0]
            for k in range(NF):
                mm3(ps.h[0:8, :], gwh.h[:, 8 * k:8 * (k + 1)],
                    gwl.h[:, 8 * k:8 * (k + 1)],
                    xhh[k].h[:, 512 * t:512 * (t + 1)],
                    xll[k].h[:, 512 * t:512 * (t + 1)], k == 0, k == NF - 1,
                    [gwh, gwl, xhh[k], xll[k]], [ps])
            s_cp(gsc.h[0:8, :], ps.h[0:8, :], [ps], [gsc])
            for si in range(4):
                sc = 4 * t + si
                ps2 = PS[1]
                t_tr(ps2.h[0:128, 0:8], gsc.h[0:8, 128 * si:128 * (si + 1)],
                     [gsc], [ps2], idw=8)
                s_cp(r8.h[:, :], ps2.h[0:128, 0:8], [ps2], [r8])
                v_max(rcol.h[:, 0:1], r8.h[:, :], [r8], [rcol])
                v_ts(r8b.h[:, :], r8.h[:, :], rcol.h[:, 0:1], -1e30,
                     ALU.is_equal, ALU.mult, [r8, rcol], [r8b])
                v_tt(ALU.add, r8b.h[:, :], r8.h[:, :], r8b.h[:, :],
                     [r8, r8b], [r8b])
                v_max(rcol.h[:, 1:2], r8b.h[:, :], [r8b], [rcol])
                v_tt(ALU.subtract, rcol.h[:, 2:3], rcol.h[:, 1:2],
                     rcol.h[:, 0:1], [rcol], [rcol])
                s_act(rcol.h[:, 2:3], rcol.h[:, 2:3], AF.Exp, [rcol], [rcol])
                v_ts(rcol.h[:, 3:4], rcol.h[:, 2:3], 1.0, None, ALU.add,
                     ALU.bypass, [rcol], [rcol])
                v_rec(rcol.h[:, 3:4], rcol.h[:, 3:4], [rcol], [rcol])
                v_tt(ALU.mult, rcol.h[:, 2:3], rcol.h[:, 2:3],
                     rcol.h[:, 3:4], [rcol], [rcol])
                v_ts(wtm.h[:, 0:8], r8.h[:, :], rcol.h[:, 0:1],
                     rcol.h[:, 3:4], ALU.is_equal, ALU.mult,
                     [r8, rcol], [wtm])
                v_ts(r8b.h[:, :], r8b.h[:, :], rcol.h[:, 1:2],
                     rcol.h[:, 2:3], ALU.is_equal, ALU.mult,
                     [r8b, rcol], [r8b])
                v_tt(ALU.add, wtm.h[:, 0:8], wtm.h[:, 0:8], r8b.h[:, :],
                     [wtm, r8b], [wtm])
                ps3 = PS[2]
                t_tr(ps3.h[0:8, 0:128], wtm.h[:, 0:8], [wtm], [ps3])
                s_cp(wte.h[0:8, 128 * sc:128 * (sc + 1)], ps3.h[0:8, 0:128],
                     [ps3], [wte])
        dma(ch_sp_wte, wte_s[0:8, s0:s0 + SH], wte.h[0:8, :], [wte], [wtes_t])
        for e in range(NE):
            dma(ch_rb, routeb.h[:, :],
                wte_s[e:e + 1, s0:s0 + SH].partition_broadcast(128),
                [wtes_t], [routeb])
            for mh in range(2):
                for mi in range(8):
                    m = 8 * mh + mi
                    b1h = w1b[2 * (mi % 2)]
                    b1l = w1b[2 * (mi % 2) + 1]
                    b2h = w2b[2 * (mi % 2)]
                    b2l = w2b[2 * (mi % 2) + 1]
                    row0 = (e * 16 + m) * 128
                    dma(ch_w1b[2 * (mi % 2)], b1h.h[:, :],
                        w1h_s[row0:row0 + 128, :], [w1h_t], [b1h])
                    dma(ch_w1b[2 * (mi % 2) + 1], b1l.h[:, :],
                        w1l_s[row0:row0 + 128, :], [w1l_t], [b1l])
                    dma(ch_w2b[2 * (mi % 2)], b2h.h[:, :],
                        w2h_s[row0:row0 + 128, :], [w2h_t], [b2h])
                    dma(ch_w2b[2 * (mi % 2) + 1], b2l.h[:, :],
                        w2l_s[row0:row0 + 128, :], [w2l_t], [b2l])
                    for t in range(NTH):
                        p1 = PS[0 + t % 2]
                        p2 = PS[2 + t % 2]
                        for k in range(NF):
                            mm3(p1.h[:, :], b1h.h[:, 128 * k:128 * (k + 1)],
                                b1l.h[:, 128 * k:128 * (k + 1)],
                                xhh[k].h[:, 512 * t:512 * (t + 1)],
                                xll[k].h[:, 512 * t:512 * (t + 1)],
                                k == 0, k == NF - 1,
                                [b1h, b1l, xhh[k], xll[k]], [p1])
                        for k in range(NF):
                            mm3(p2.h[:, :], b2h.h[:, 128 * k:128 * (k + 1)],
                                b2l.h[:, 128 * k:128 * (k + 1)],
                                xhh[k].h[:, 512 * t:512 * (t + 1)],
                                xll[k].h[:, 512 * t:512 * (t + 1)],
                                k == 0, k == NF - 1,
                                [b2h, b2l, xhh[k], xll[k]], [p2])
                        s_act(silu_b.h[:, :], p1.h[:, :], AF.Silu,
                              [p1], [silu_b])
                        hft = hf[t % 2]
                        v_tt(ALU.mult, hft.h[:, :],
                             silu_b.h[:, :], p2.h[:, :], [silu_b, p2], [hft])
                        s_cp(heh[mi].h[:, 512 * t:512 * (t + 1)], hft.h[:, :],
                             [hft], [heh[mi]])
                        v_tt(ALU.subtract,
                             hel[mi].h[:, 512 * t:512 * (t + 1)], hft.h[:, :],
                             heh[mi].h[:, 512 * t:512 * (t + 1)],
                             [hft, heh[mi]], [hel[mi]])
                for fh in range(2):
                    for ki in range(8):
                        k = 8 * mh + ki
                        wbh = w3hb[ki % 3]
                        wbl = w3lb[ki % 3]
                        dma(ch_w3[ki % 3], wbh.h[:, :],
                            w3h_s[e * C + 128 * k:e * C + 128 * (k + 1),
                                  512 * fh:512 * (fh + 1)], [w3h_t], [wbh])
                        dma(ch_w3[ki % 3], wbl.h[:, :],
                            w3l_s[e * C + 128 * k:e * C + 128 * (k + 1),
                                  512 * fh:512 * (fh + 1)], [w3l_t], [wbl])
                        for t in range(NTH):
                            for f in range(4):
                                mm3(PS[4 * t + f].h[:, :],
                                    wbh.h[:, 128 * f:128 * (f + 1)],
                                    wbl.h[:, 128 * f:128 * (f + 1)],
                                    heh[ki].h[:, 512 * t:512 * (t + 1)],
                                    hel[ki].h[:, 512 * t:512 * (t + 1)],
                                    ki == 0, ki == 7,
                                    [wbh, wbl, heh[ki], hel[ki]],
                                    [PS[4 * t + f]])
                    for t in range(NTH):
                        for f in range(4):
                            fi = 4 * fh + f
                            v_tt(ALU.mult, tmpb.h[:, :],
                                 PS[4 * t + f].h[:, :],
                                 routeb.h[:, 512 * t:512 * (t + 1)],
                                 [PS[4 * t + f], routeb], [tmpb])
                            v_tt(ALU.add,
                                 acc[fi].h[:, 512 * t:512 * (t + 1)],
                                 acc[fi].h[:, 512 * t:512 * (t + 1)],
                                 tmpb.h[:, :], [acc[fi], tmpb], [acc[fi]])
        for t in range(NTH):
            w0 = s0 + 512 * t
            for k in range(NF):
                dma(ch_ar2[k], ar2[k].h[:, :],
                    a_s[128 * k:128 * (k + 1), w0:w0 + 512], [as_t], [ar2[k]])
            for k in range(NF):
                v_tt(ALU.add, ar2[k].h[:, :], ar2[k].h[:, :],
                     acc[k].h[:, 512 * t:512 * (t + 1)],
                     [ar2[k], acc[k]], [ar2[k]])
            ln_fm([(ar2[k], ar2[k].h[:, :]) for k in range(NF)],
                  lambda k: (ar2[k], ar2[k].h[:, :]), 512)
            for k in range(NF):
                xr = x1r2[k % 2]
                dma(ch_x1r2[k % 2], xr.h[:, :],
                    x1_s[128 * k:128 * (k + 1), w0:w0 + 512], [x1s_t], [xr])
                v_tt(ALU.add, ar2[k].h[:, :], ar2[k].h[:, :], xr.h[:, :],
                     [ar2[k], xr], [ar2[k]])
            for sc in range(4):
                for kh2 in range(2):
                    for kk in range(4):
                        k = 4 * kh2 + kk
                        ps = PS[k % 2]
                        t_tr(ps.h[:, 0:128],
                             ar2[k].h[:, 128 * sc:128 * (sc + 1)],
                             [ar2[k]], [ps])
                        s_cp(outst.h[:, 128 * kk:128 * (kk + 1)],
                             ps.h[:, 0:128], [ps], [outst])
                    dma(ch_out,
                        out[w0 + 128 * sc:w0 + 128 * (sc + 1),
                            512 * kh2:512 * (kh2 + 1)],
                        outst.h[:, :], [outst], [out_t])

    em.emit("sync", None, reads=[out_t], writes=[])
    em.finalize()
    es.close()
    return nc


def _prep(inputs):
    ct = np.ascontiguousarray
    f32 = lambda a: np.asarray(a, dtype=np.float32)

    def head_repack(w):
        return ct(w.reshape(NF, 128, NH, HD).transpose(2, 1, 0, 3)
                  .reshape(NH * 128, NF * HD))

    def w12_repack(w):
        return ct(w.reshape(NE, NF, 128, NC_, 128).transpose(0, 3, 2, 1, 4)
                  .reshape(NE * NC_ * 128, NF * 128))

    def vchunks(v, n):
        return ct(v.reshape(n, 128).T)

    conv_w = f32(inputs["L0_attn_conv_w"])
    vecs = np.zeros((128, 48), np.float32)
    vecs[:, 0:16] = vchunks(f32(inputs["L0_attn_conv_b"]), 16)
    vecs[:, 16:32] = vchunks(f32(inputs["L0_attn_b_dt"]), 16)
    vecs[:, 32:48] = vchunks(f32(inputs["L0_attn_D"]), 16)
    vecd = np.zeros((128, 40), np.float32)
    vecd[:, 0:8] = vchunks(f32(inputs["L0_ffn_ln1_scale"]), 8)
    vecd[:, 8:16] = vchunks(f32(inputs["L0_ffn_ln1_shift"]), 8)
    vecd[:, 16:24] = vchunks(f32(inputs["L0_ffn_ln2_scale"]), 8)
    vecd[:, 24:32] = vchunks(f32(inputs["L0_ffn_ln2_shift"]), 8)
    vecd[:, 32:40] = vchunks(f32(inputs["L0_ffn_bo"]), 8)
    cw_r = ct(conv_w[:, 0, :].T.reshape(16, 128, 4).transpose(1, 0, 2)
              .reshape(128, 64))
    alog_r = ct(f32(inputs["L0_attn_A_log"]).reshape(16, 128, 16)
                .transpose(1, 0, 2).reshape(128, 256))
    gw_r = ct(f32(inputs["L0_ffn_gate_w"]).reshape(8, 128, 8)
              .transpose(1, 0, 2).reshape(128, 64))
    masks = np.zeros((128, 4 * 512), np.float32)
    i = np.arange(128)[:, None]
    j = np.arange(512)[None, :]
    for r in range(4):
        masks[:, 512 * r:512 * (r + 1)] = (j - 128 * r >= i)

    return {
        "w_in": f32(inputs["L0_attn_W_in"]),
        "w_x": f32(inputs["L0_attn_W_x"]),
        "w_dt": f32(inputs["L0_attn_W_dt"]),
        "w_out": f32(inputs["L0_attn_W_out"]),
        "wq_r": head_repack(f32(inputs["L0_ffn_wq"])),
        "wk_r": head_repack(f32(inputs["L0_ffn_wk"])),
        "wv": f32(inputs["L0_ffn_wv"]),
        "wo": f32(inputs["L0_ffn_wo"]),
        "w1_r": w12_repack(f32(inputs["L0_ffn_w1"])),
        "w2_r": w12_repack(f32(inputs["L0_ffn_w2"])),
        "w3": ct(f32(inputs["L0_ffn_w3"]).reshape(NE * C, D)),
        "vecs": vecs, "vecd": vecd, "cw_r": cw_r, "alog_r": alog_r,
        "gw_r": gw_r, "ident": np.eye(128, dtype=np.float32), "masks": masks,
    }


def kernel(**inputs):
    S = 2048
    x = np.asarray(inputs["L0_attn_x"], dtype=np.float32)
    B = x.shape[0]
    shared = _prep(inputs)
    nc = build_program(S)
    in_maps = [dict(shared, x=np.ascontiguousarray(x[i])) for i in range(B)]
    res = run_bass_kernel_spmd(nc, in_maps, core_ids=list(range(B)))
    return np.stack([np.asarray(r["out"]) for r in res.results], axis=0)


# version 05
# speedup vs baseline: 1.5842x; 1.5842x over previous
import numpy as np
from contextlib import ExitStack

import concourse.bass as bass
import concourse.mybir as mybir
from concourse.bass_utils import run_bass_kernel_spmd

F32 = mybir.dt.float32
BF16 = mybir.dt.bfloat16
U8 = mybir.dt.uint8
AF = mybir.ActivationFunctionType
ALU = mybir.AluOpType
AX = mybir.AxisListType

ENGS = ("sync", "tensor", "scalar", "vector", "gpsimd")
PROG_ENGS = ("tensor", "scalar", "vector", "gpsimd")

D = 1024
C = 2048
NF = D // 128
NC_ = C // 128
NJ = 16
NH = 16
HD = 64
NE = 8
EPS = 1e-5


class Chan:
    __slots__ = ("sem", "n", "name")

    def __init__(self, sem, name):
        self.sem = sem
        self.n = 0
        self.name = name


TILE_NAMES = {}


class Tile:
    __slots__ = ("h", "name", "w", "rd")

    def __init__(self, h, name):
        self.h = h
        self.name = name
        self.w = None
        self.rd = {}
        try:
            TILE_NAMES[h.name] = name
        except Exception:
            pass


class Em:
    def __init__(self, nc, es):
        self.nc = nc
        self.es = es
        self.q = {e: [] for e in ENGS}
        self.idx = {e: 0 for e in ENGS}
        self.need = {e: set() for e in ENGS}
        self.prog = {e: es.enter_context(nc.semaphore("prog_" + e))
                     for e in PROG_ENGS}
        self.chans = []
        self.pending = {}

    def chan(self, name):
        if not hasattr(self, "_chan_cache"):
            self._chan_cache = {}
        if name in self._chan_cache:
            return self._chan_cache[name]
        ch = Chan(self.es.enter_context(self.nc.semaphore("ch_" + name)), name)
        self.chans.append(ch)
        self._chan_cache[name] = ch
        return ch

    def _rdep(self, t):
        k, a, v = t.w
        if k == "d":
            return ("d", a, a.n)
        return t.w

    def emit(self, eng, build, reads=(), writes=(), dma=None):
        deps = []
        for t in reads:
            if t.w is not None:
                deps.append(self._rdep(t))
        for t in writes:
            if t.w is not None:
                deps.append(self._rdep(t))
            deps.extend(t.rd.values())
        if self.pending.get(eng):
            deps.extend(self.pending[eng])
            self.pending[eng] = None
        best = {}
        for d in deps:
            if d[0] == "e":
                if d[1] == eng and eng == "tensor":
                    continue
                key = ("e", d[1])
            else:
                key = ("d", id(d[1]))
            if key not in best or d[2] > best[key][2]:
                best[key] = d
        fdeps = list(best.values())
        for d in fdeps:
            if d[0] == "e":
                self.need[d[1]].add(d[2])
        if dma is not None:
            dma.n += 16
            tag = ("d", dma, dma.n)
            self.q[eng].append((fdeps, build, dma, None))
        else:
            self.idx[eng] += 1
            tag = ("e", eng, self.idx[eng])
            self.q[eng].append((fdeps, build, None, self.idx[eng]))
        for t in writes:
            t.w = tag
            t.rd = {}
        rkey = ("d", id(dma)) if dma is not None else ("e", eng)
        for t in reads:
            t.rd[rkey] = tag
        return tag

    def barrier(self):
        deps = [("e", e, self.idx[e]) for e in PROG_ENGS if self.idx[e] > 0]
        deps += [("d", ch, ch.n) for ch in self.chans if ch.n > 0]
        for e in ENGS:
            self.pending[e] = list(deps)
        for d in deps:
            if d[0] == "e":
                self.need[d[1]].add(d[2])

    def finalize(self):
        rank = {e: {i: r + 1 for r, i in enumerate(sorted(self.need[e]))}
                for e in PROG_ENGS}

        def runner(eng):
            def run(e):
                known = {}
                for fdeps, build, dma, myidx in self.q[eng]:
                    for d in fdeps:
                        if d[0] == "e":
                            tgt = rank[d[1]][d[2]]
                            sem = self.prog[d[1]]
                            key = ("e", d[1])
                        else:
                            tgt = d[2]
                            sem = d[1].sem
                            key = ("d", id(d[1]))
                        if known.get(key, 0) < tgt:
                            e.wait_ge(sem, tgt)
                            known[key] = tgt
                    if build is None:
                        continue
                    ins = build(e)
                    if dma is not None:
                        ins.then_inc(dma.sem, 16)
                    elif myidx in self.need[eng]:
                        ins.then_inc(self.prog[eng], 1)
            return run

        with self.nc.Block() as block:
            for eng in ENGS:
                getattr(block, eng)(runner(eng))


class SB:
    def __init__(self, nc, total):
        self.nc = nc
        self.h = nc.alloc_sbuf_tensor("arena", [128, total], U8)
        self.base = nc.lookup_mloc(self.h).addr
        self.total = total
        self.off = 0
        self.ph_base = None
        self.ph_off = 0
        self.cnt = 0

    def _mk(self, shape, dtype, off):
        nbytes = int(np.prod(shape[1:])) * mybir.dt.size(dtype)
        assert off + nbytes <= self.total, \
            ("sbuf overflow", off, nbytes, self.total)
        self.cnt += 1
        return self.nc.alloc_sbuf_tensor_at(
            "sb%d" % self.cnt, list(shape), dtype, offset=self.base + off)

    def persist(self, shape, dtype=F32):
        assert self.ph_base is None
        off = (self.off + 31) & ~31
        h = self._mk(shape, dtype, off)
        self.off = off + int(np.prod(shape[1:])) * mybir.dt.size(dtype)
        return h

    def phase_begin(self):
        if self.ph_base is None:
            self.ph_base = (self.off + 127) & ~127
        self.ph_off = self.ph_base

    def mark(self):
        return self.ph_off

    def reset(self, mk):
        self.ph_off = mk

    def alloc(self, shape, dtype=F32):
        off = (self.ph_off + 31) & ~31
        h = self._mk(shape, dtype, off)
        self.ph_off = off + int(np.prod(shape[1:])) * mybir.dt.size(dtype)
        self.last_rng = (off, self.ph_off)
        return h


def build_program(S):
    NQ = S // 512
    SQ = 512
    SH = S // 2
    NKT = S // 128

    nc = bass.Bass("TRN2", target_bir_lowering=False, debug=False)
    es = ExitStack()
    em = Em(nc, es)

    def dram(name, shape, kind="Internal"):
        return nc.dram_tensor(name, list(shape), F32, kind=kind)

    x_in = dram("x", (S, D), "ExternalInput")
    out = dram("out", (S, D), "ExternalOutput")
    w_in = dram("w_in", (D, 2 * C), "ExternalInput")
    w_x = dram("w_x", (C, 96), "ExternalInput")
    w_dt = dram("w_dt", (64, C), "ExternalInput")
    w_out = dram("w_out", (C, D), "ExternalInput")
    wq_r = dram("wq_r", (NH * 128, NF * HD), "ExternalInput")
    wk_r = dram("wk_r", (NH * 128, NF * HD), "ExternalInput")
    wv_d = dram("wv", (D, D), "ExternalInput")
    wo = dram("wo", (D, D), "ExternalInput")
    w1_r = dram("w1_r", (NE * 16 * 128, 8 * 128), "ExternalInput")
    w2_r = dram("w2_r", (NE * 16 * 128, 8 * 128), "ExternalInput")
    w3_2d = dram("w3", (NE * C, D), "ExternalInput")
    vecs = dram("vecs", (128, 48), "ExternalInput")
    vecd_d = dram("vecd", (128, 40), "ExternalInput")
    cw_d = dram("cw_r", (128, 64), "ExternalInput")
    alog_d = dram("alog_r", (128, 16 * NC_), "ExternalInput")
    gw_d = dram("gw_r", (128, 64), "ExternalInput")
    ident_d = dram("ident", (128, 128), "ExternalInput")
    masks_d = dram("masks", (128, 4 * 512), "ExternalInput")
    h_s = dram("h_s", (C, S))
    gate_s = dram("gate_s", (C, S))
    x0_s = dram("x0_s", (D, S))
    x1_s = dram("x1_s", (D, S))
    g_s = dram("g_s", (D, S))
    a_s = dram("a_s", (D, S))
    xln2_s = dram("xln2_s", (D, S))
    ctx_s = dram("ctx_s", (D, S))
    bc_s = dram("bc_s", (32, S))
    wte_s = dram("wte_s", (8, S))
    dramb = lambda name, shape: nc.dram_tensor(name, list(shape), BF16,
                                               kind="Internal")
    w1h_s = dramb("w1h_s", (NE * 16 * 128, 8 * 128))
    w1l_s = dramb("w1l_s", (NE * 16 * 128, 8 * 128))
    w2h_s = dramb("w2h_s", (NE * 16 * 128, 8 * 128))
    w2l_s = dramb("w2l_s", (NE * 16 * 128, 8 * 128))
    w3h_s = dramb("w3h_s", (NE * C, D))
    w3l_s = dramb("w3l_s", (NE * C, D))
    xl2h_s = dramb("xl2h_s", (D, S))
    xl2l_s = dramb("xl2l_s", (D, S))

    rem = nc.sbuf_bytes_remaining
    sb = SB(nc, rem - 4096)

    T = Tile
    ident = T(sb.persist([128, 128]), "ident")
    ones = T(sb.persist([128, 128]), "ones")
    cw = T(sb.persist([128, 64]), "cw")
    vec = T(sb.persist([128, 48]), "vec")
    vecd = T(sb.persist([128, 40]), "vecd")
    apb = T(sb.persist([128, 16 * NC_]), "apb")
    gw = T(sb.persist([128, 64]), "gw")
    gwh = T(sb.persist([128, 64], BF16), "gwh")
    gwl = T(sb.persist([128, 64], BF16), "gwl")
    ssm = T(sb.persist([96, S]), "ssm")
    carry = T(sb.persist([128, 16 * NC_]), "carry")
    wdt = T(sb.persist([64, C]), "wdt")
    wx = T(sb.persist([128, 96 * NC_]), "wx")
    halo = T(sb.persist([128, 3 * NC_]), "halo")
    rows = T(sb.persist([1, 3072]), "rows")
    sqr_t = [T(sb.persist([128, 512]), "lnsq%d" % i) for i in range(2)]
    lnrb = T(sb.persist([128, 512]), "lnrb")
    lnmb = T(sb.persist([128, 512]), "lnmb")

    PS = [T(nc.alloc_psum_tensor("ps%d" % i, [128, 512], F32), "ps%d" % i)
          for i in range(8)]

    alias_live = []

    def AT(shape, name, dtype=F32):
        h = sb.alloc(shape, dtype)
        lo, hi = sb.last_rng
        t = Tile(h, name)
        for (a, b, pt) in alias_live:
            if a < hi and lo < b and pt is not t:
                if pt.w is not None:
                    t.rd[("al", id(pt), "w")] = pt.w
                for k, v in pt.rd.items():
                    t.rd[("al", id(pt), k)] = v
        alias_live.append((lo, hi, t))
        return t

    def t_mm(o_ap, l_ap, r_ap, st, sp, rd, wr):
        em.emit("tensor",
                lambda e, o=o_ap, l=l_ap, r=r_ap, a=st, b=sp:
                e.matmul(o, l, r, start=a, stop=b), rd, wr)

    def t_tr(o_ap, i_ap, rd, wr, idw=128):
        em.emit("tensor",
                lambda e, o=o_ap, i=i_ap, d=ident.h[0:idw, 0:idw]:
                e.transpose(o, i, d), list(rd) + [ident], wr)

    def mm3(ps_ap, lh, ll, rh, rl, start, stop, rd, wr):
        t_mm(ps_ap, lh, rh, start, False, rd, wr)
        t_mm(ps_ap, lh, rl, False, False, rd, wr)
        t_mm(ps_ap, ll, rh, False, stop, rd, wr)

    def s_act(o_ap, i_ap, fn, rd, wr, bias=0.0, scale=1.0):
        em.emit("scalar",
                lambda e, o=o_ap, i=i_ap, f=fn, b=bias, s=scale:
                e.activation(o, i, f, bias=b, scale=s), rd, wr)

    def s_cp(o_ap, i_ap, rd, wr):
        em.emit("scalar", lambda e, o=o_ap, i=i_ap: e.copy(o, i), rd, wr)

    def v_tt(op, o_ap, a_ap, b_ap, rd, wr, eng="vector"):
        em.emit(eng, lambda e, o=o_ap, a=a_ap, b=b_ap, p=op:
                e.tensor_tensor(o, a, b, op=p), rd, wr)

    def v_ts(o_ap, i_ap, s1, s2, op0, op1, rd, wr, eng="vector"):
        em.emit(eng, lambda e, o=o_ap, i=i_ap, a=s1, b=s2, p0=op0, p1=op1:
                e.tensor_scalar(o, i, a, b, op0=p0, op1=p1), rd, wr)

    def v_stt(o_ap, i0_ap, sc, i1_ap, op0, op1, rd, wr, eng="vector"):
        em.emit(eng, lambda e, o=o_ap, i0=i0_ap, s=sc, i1=i1_ap, p0=op0, p1=op1:
                e.scalar_tensor_tensor(o, i0, s, i1, op0=p0, op1=p1), rd, wr)

    def v_scan(o_ap, d0_ap, d1_ap, init, rd, wr):
        em.emit("vector", lambda e, o=o_ap, a=d0_ap, b=d1_ap, i=init:
                e.tensor_tensor_scan(o, a, b, i, op0=ALU.mult, op1=ALU.add),
                rd, wr)

    def v_cp(o_ap, i_ap, rd, wr, eng="vector"):
        em.emit(eng, lambda e, o=o_ap, i=i_ap: e.tensor_copy(o, i), rd, wr)

    def v_max(o_ap, i_ap, rd, wr):
        em.emit("vector", lambda e, o=o_ap, i=i_ap:
                e.reduce_max(o, i, axis=AX.X), rd, wr)

    def v_rec(o_ap, i_ap, rd, wr):
        em.emit("vector", lambda e, o=o_ap, i=i_ap: e.reciprocal(o, i), rd, wr)

    def v_ms(ap, val, t, eng="vector"):
        em.emit(eng, lambda e, o=ap, v=val: e.memset(o, v), [], [t])

    def dma(ch, o_ap, i_ap, rd, wr, eng="sync"):
        em.emit(eng, lambda e, o=o_ap, i=i_ap: e.dma_start(out=o, in_=i),
                rd, wr, dma=ch)

    def ln_fm(srcs, dst_fn, W, scale=None, shift=None, par_t=None, res=None,
              psA=None, psB=None):
        psA = psA or PS[6]
        psB = psB or PS[7]
        nk = len(srcs)
        for k, (t, ap) in enumerate(srcs):
            t_mm(psA.h[0:1, 0:W], ones.h[:, 0:1], ap, k == 0, k == nk - 1,
                 [t, ones], [psA])
        for k, (t, ap) in enumerate(srcs):
            sq = sqr_t[k % 2]
            s_act(sq.h[:, 0:W], ap, AF.Square, [t], [sq])
            t_mm(psB.h[0:1, 0:W], ones.h[:, 0:1], sq.h[:, 0:W], k == 0,
                 k == nk - 1, [sq, ones], [psB])
        mu = rows.h[0:1, 0:W]
        m2 = rows.h[0:1, 512:512 + W]
        veps = rows.h[0:1, 1024:1024 + W]
        rstd = rows.h[0:1, 1536:1536 + W]
        mur = rows.h[0:1, 2048:2048 + W]
        tmp = rows.h[0:1, 2560:2560 + W]
        inv = 1.0 / D
        v_ts(mu, psA.h[0:1, 0:W], inv, None, ALU.mult, ALU.bypass, [psA], [rows])
        v_ts(m2, psB.h[0:1, 0:W], inv, None, ALU.mult, ALU.bypass, [psB], [rows])
        v_tt(ALU.mult, tmp, mu, mu, [rows], [rows])
        v_tt(ALU.subtract, tmp, m2, tmp, [rows], [rows])
        v_ts(veps, tmp, EPS, None, ALU.add, ALU.bypass, [rows], [rows])
        s_act(rstd, veps, AF.Sqrt, [rows], [rows])
        v_rec(rstd, rstd, [rows], [rows])
        v_tt(ALU.mult, tmp, rstd, rstd, [rows], [rows])
        v_tt(ALU.mult, tmp, tmp, veps, [rows], [rows])
        v_ts(tmp, tmp, -0.5, 1.5, ALU.mult, ALU.add, [rows], [rows])
        v_tt(ALU.mult, rstd, rstd, tmp, [rows], [rows])
        v_tt(ALU.mult, mur, mu, rstd, [rows], [rows])
        t_mm(psA.h[:, 0:W], ones.h[0:1, :], rstd, True, True, [rows, ones], [psA])
        t_mm(psB.h[:, 0:W], ones.h[0:1, :], mur, True, True, [rows, ones], [psB])
        s_cp(lnrb.h[:, 0:W], psA.h[:, 0:W], [psA], [lnrb])
        s_cp(lnmb.h[:, 0:W], psB.h[:, 0:W], [psB], [lnmb])
        for k, (t, ap) in enumerate(srcs):
            dt_, dap = dst_fn(k)
            v_tt(ALU.mult, dap, ap, lnrb.h[:, 0:W], [t, lnrb], [dt_])
            v_tt(ALU.subtract, dap, dap, lnmb.h[:, 0:W], [dt_, lnmb], [dt_])
            if scale is not None:
                v_ts(dap, dap, scale(k), shift(k), ALU.mult, ALU.add,
                     [dt_, par_t], [dt_])
            if res is not None:
                rt, rap = res(k)
                v_tt(ALU.add, dap, dap, rap, [dt_, rt], [dt_])

    ch_const = em.chan("const")
    dma(ch_const, ident.h[:, :], ident_d[:, :], [], [ident])
    dma(ch_const, cw.h[:, :], cw_d[:, :], [], [cw])
    dma(ch_const, vec.h[:, :], vecs[:, :], [], [vec])
    dma(ch_const, vecd.h[:, :], vecd_d[:, :], [], [vecd])
    dma(ch_const, apb.h[:, :], alog_d[:, :], [], [apb])
    dma(ch_const, gw.h[:, :], gw_d[:, :], [], [gw])
    dma(ch_const, wdt.h[:, :], w_dt[:, :], [], [wdt])
    for c in range(NC_):
        dma(ch_const, wx.h[:, 96 * c:96 * (c + 1)],
            w_x[128 * c:128 * (c + 1), :], [], [wx])
    v_ms(ones.h[:, :], 1.0, ones)
    s_cp(gwh.h[:, :], gw.h[:, :], [gw], [gwh])
    v_tt(ALU.subtract, gwl.h[:, :], gw.h[:, :], gwh.h[:, :], [gw, gwh], [gwl])
    s_act(apb.h[:, :], apb.h[:, :], AF.Exp, [apb], [apb])
    v_ts(apb.h[:, :], apb.h[:, :], -1.0, None, ALU.mult, ALU.bypass,
         [apb], [apb])
    for c in range(NC_):
        v_ms(halo.h[:, 3 * c:3 * (c + 1)], 0.0, halo)

    CB = lambda c: vec.h[:, c:c + 1]
    BDT = lambda c: vec.h[:, 16 + c:17 + c]
    DV = lambda c: vec.h[:, 32 + c:33 + c]
    LN1S = lambda k: vecd.h[:, k:k + 1]
    LN1B = lambda k: vecd.h[:, 8 + k:9 + k]
    LN2S = lambda k: vecd.h[:, 16 + k:17 + k]
    LN2B = lambda k: vecd.h[:, 24 + k:25 + k]
    BO = lambda k: vecd.h[:, 32 + k:33 + k]

    sb.phase_begin()
    x_tm = [AT([128, D], "x_tm%d" % i) for i in range(2)]
    ch_xtm = [em.chan("xtm%d" % i) for i in range(2)]
    x0q = [AT([128, SQ], "x0q%d" % k) for k in range(NF)]
    xlnq = [AT([128, SQ], "xlnq%d" % k) for k in range(NF)]
    winh = [AT([128, C], "winh%d" % k) for k in range(NF)]
    ch_win = [em.chan("win")] * NF
    rawx = [AT([128, 515], "rawx%d" % i) for i in range(2)]
    hbuf = [AT([128, 512], "hbuf%d" % i) for i in range(2)]
    gbuf = [AT([128, 512], "gbuf%d" % i) for i in range(2)]
    ch_sp_h = em.chan("sp_h")
    ch_sp_g = em.chan("sp_g")
    ch_sp_x0 = em.chan("sp_x0")
    hs_t = T(h_s, "h_s")
    gs_t = T(gate_s, "gate_s")
    x0s_t = T(x0_s, "x0_s")

    NSQ = S // SQ
    NT = SQ // 512
    for q in range(NSQ):
        for scn in range(SQ // 128):
            xt = x_tm[scn % 2]
            s0 = q * SQ + scn * 128
            dma(ch_xtm[scn % 2], xt.h[:, :], x_in[s0:s0 + 128, :], [], [xt])
            for k in range(NF):
                ps = PS[k % 2]
                t_tr(ps.h[:, 0:128], xt.h[:, 128 * k:128 * (k + 1)], [xt], [ps])
                s_cp(x0q[k].h[:, 128 * scn:128 * (scn + 1)], ps.h[:, 0:128],
                     [ps], [x0q[k]])
        for t in range(NT):
            w0 = 512 * t
            ln_fm([(x0q[k], x0q[k].h[:, w0:w0 + 512]) for k in range(NF)],
                  lambda k, w0=w0: (xlnq[k], xlnq[k].h[:, w0:w0 + 512]), 512)
        for k in range(NF):
            dma(ch_sp_x0, x0_s[128 * k:128 * (k + 1), q * SQ:(q + 1) * SQ],
                x0q[k].h[:, :], [x0q[k]], [x0s_t])
        for half in range(2):
            for k in range(NF):
                dma(ch_win[k], winh[k].h[:, :],
                    w_in[128 * k:128 * (k + 1), half * C:(half + 1) * C],
                    [], [winh[k]])
            for m in range(NC_):
                for t in range(NT):
                    ps = PS[2 + (m + t) % 2]
                    for k in range(NF):
                        t_mm(ps.h[:, :], winh[k].h[:, 128 * m:128 * (m + 1)],
                             xlnq[k].h[:, 512 * t:512 * (t + 1)],
                             k == 0, k == NF - 1, [winh[k], xlnq[k]], [ps])
                    scol = q * SQ + 512 * t
                    if half == 0:
                        c = m
                        rx = rawx[(m + t) % 2]
                        v_cp(rx.h[:, 0:3], halo.h[:, 3 * c:3 * (c + 1)],
                             [halo], [rx])
                        s_cp(rx.h[:, 3:515], ps.h[:, :], [ps], [rx])
                        v_cp(halo.h[:, 3 * c:3 * (c + 1)], rx.h[:, 512:515],
                             [rx], [halo])
                        hb = hbuf[(m + t) % 2]
                        v_ts(hb.h[:, :], rx.h[:, 0:512],
                             cw.h[:, 4 * c:4 * c + 1], None, ALU.mult,
                             ALU.bypass, [rx, cw], [hb])
                        for kk in range(1, 4):
                            v_stt(hb.h[:, :], rx.h[:, kk:kk + 512],
                                  cw.h[:, 4 * c + kk:4 * c + kk + 1],
                                  hb.h[:, :], ALU.mult, ALU.add,
                                  [rx, cw, hb], [hb])
                        s_act(hb.h[:, :], hb.h[:, :], AF.Silu, [hb, vec], [hb],
                              bias=CB(c))
                        psm = PS[4 + t % 2]
                        t_mm(psm.h[0:96, :], wx.h[:, 96 * c:96 * (c + 1)],
                             hb.h[:, :], c == 0, c == NC_ - 1, [wx, hb], [psm])
                        if c == NC_ - 1:
                            s_cp(ssm.h[:, scol:scol + 512], psm.h[0:96, :],
                                 [psm], [ssm])
                        dma(ch_sp_h, h_s[128 * c:128 * (c + 1), scol:scol + 512],
                            hb.h[:, :], [hb], [hs_t])
                    else:
                        gb = gbuf[(m + t) % 2]
                        s_cp(gb.h[:, :], ps.h[:, :], [ps], [gb])
                        dma(ch_sp_g,
                            gate_s[128 * m:128 * (m + 1), scol:scol + 512],
                            gb.h[:, :], [gb], [gs_t])

    ch_sp_bc = em.chan("sp_bc")
    bcs_t = T(bc_s, "bc_s")
    dma(ch_sp_bc, bc_s[:, :], ssm.h[64:96, :], [ssm], [bcs_t])

    sb.phase_begin()
    NCQ = 4
    dtq = [AT([128, 512], "dtq%d" % i) for i in range(NCQ)]
    dthq = [AT([128, 512], "dthq%d" % i) for i in range(NCQ)]
    hq = [AT([128, 512], "hq%d" % i) for i in range(NCQ)]
    ch_hq = [em.chan("hq%d" % i) for i in range(NCQ)]
    yacc = [AT([128, 512], "yacc%d" % i) for i in range(NCQ)]
    yq = [AT([128, 512], "yq%d" % c) for c in range(NC_)]
    bb = [AT([128, 512], "bb%d" % i) for i in range(2)]
    cbb = [AT([128, 512], "cbb%d" % i) for i in range(2)]
    ch_bb = [em.chan("bbc%d" % i) for i in range(2)]
    ch_cb = [em.chan("cbc%d" % i) for i in range(2)]
    dA = [AT([128, 512], "dA%d" % i) for i in range(4)]
    dBu = [AT([128, 512], "dBu%d" % i) for i in range(4)]
    stl = [AT([128, 512], "st%d" % i) for i in range(4)]
    gq = [AT([128, 512], "gq%d" % i) for i in range(2)]
    sp1 = AT([128, 512], "sp1")
    sp2 = AT([128, 512], "sp2")
    ch_gq = [em.chan("gq%d" % i) for i in range(2)]
    sgq = AT([128, 512], "sgq")
    woutb = [AT([128, D], "woutb%d" % i) for i in range(2)]
    ch_wout = [em.chan("wout%d" % i) for i in range(2)]
    ast = [AT([128, 512], "ast%d" % k) for k in range(NF)]
    x0r = [AT([128, 512], "x0r%d" % i) for i in range(2)]
    ch_x0r = [em.chan("x0r%d" % i) for i in range(2)]
    ch_sp_x1 = em.chan("sp_x1")
    x1s_t = T(x1_s, "x1_s")

    for q in range(NSQ):
        s0 = q * SQ
        for cq in range(NC_ // NCQ):
            cbase = cq * NCQ
            for ci in range(NCQ):
                c = cbase + ci
                ps = PS[ci % 2]
                t_mm(ps.h[:, :], wdt.h[0:64, 128 * c:128 * (c + 1)],
                     ssm.h[0:64, s0:s0 + 512], True, True, [wdt, ssm], [ps])
                s_act(sp1.h[:, :], ps.h[:, :], AF.Abs, [ps, vec], [sp1],
                      bias=BDT(c))
                s_act(sp2.h[:, :], sp1.h[:, :], AF.Exp, [sp1], [sp2],
                      scale=-1.0)
                s_act(sp2.h[:, :], sp2.h[:, :], AF.Ln, [sp2], [sp2], bias=1.0)
                s_act(sp1.h[:, :], ps.h[:, :], AF.Relu, [ps, vec], [sp1],
                      bias=BDT(c))
                v_tt(ALU.add, dtq[ci].h[:, :], sp1.h[:, :], sp2.h[:, :],
                     [sp1, sp2], [dtq[ci]])
                dma(ch_hq[ci], hq[ci].h[:, :],
                    h_s[128 * c:128 * (c + 1), s0:s0 + 512], [hs_t], [hq[ci]])
                v_tt(ALU.mult, dthq[ci].h[:, :], dtq[ci].h[:, :],
                     hq[ci].h[:, :], [dtq[ci], hq[ci]], [dthq[ci]])
            for j in range(NJ):
                bbt = bb[j % 2]
                cbt = cbb[j % 2]
                dma(ch_bb[j % 2], bbt.h[:, :],
                    bc_s[j:j + 1, s0:s0 + 512].partition_broadcast(128),
                    [bcs_t], [bbt])
                dma(ch_cb[j % 2], cbt.h[:, :],
                    bc_s[16 + j:17 + j, s0:s0 + 512].partition_broadcast(128),
                    [bcs_t], [cbt])
                for ci in range(NCQ):
                    c = cbase + ci
                    cj = 16 * c + j
                    da = dA[ci]
                    db = dBu[ci]
                    st_t = stl[ci]
                    s_act(da.h[:, :], dtq[ci].h[:, :], AF.Exp,
                          [dtq[ci], apb], [da],
                          scale=apb.h[:, cj:cj + 1])
                    v_tt(ALU.mult, db.h[:, :], dthq[ci].h[:, :], bbt.h[:, :],
                         [dthq[ci], bbt], [db])
                    init = 0.0 if q == 0 else carry.h[:, cj:cj + 1]
                    rd = [da, db] + ([carry] if q > 0 else [])
                    v_scan(st_t.h[:, :], da.h[:, :], db.h[:, :], init,
                           rd, [st_t])
                    if q < NSQ - 1:
                        v_cp(carry.h[:, cj:cj + 1], st_t.h[:, 511:512],
                             [st_t], [carry])
                    v_tt(ALU.mult, da.h[:, :], st_t.h[:, :], cbt.h[:, :],
                         [st_t, cbt], [da])
                    if j == 0:
                        v_cp(yacc[ci].h[:, :], da.h[:, :], [da], [yacc[ci]])
                    else:
                        v_tt(ALU.add, yacc[ci].h[:, :], yacc[ci].h[:, :],
                             da.h[:, :], [yacc[ci], da], [yacc[ci]])
            for ci in range(NCQ):
                c = cbase + ci
                gt = gq[ci % 2]
                dma(ch_gq[ci % 2], gt.h[:, :],
                    gate_s[128 * c:128 * (c + 1), s0:s0 + 512], [gs_t], [gt])
                s_act(sgq.h[:, :], gt.h[:, :], AF.Silu, [gt], [sgq])
                v_stt(yacc[ci].h[:, :], hq[ci].h[:, :], DV(c),
                      yacc[ci].h[:, :], ALU.mult, ALU.add,
                      [hq[ci], vec, yacc[ci]], [yacc[ci]])
                v_tt(ALU.mult, yq[c].h[:, :], yacc[ci].h[:, :], sgq.h[:, :],
                     [yacc[ci], sgq], [yq[c]])
        for k in range(NC_):
            wb = woutb[k % 2]
            dma(ch_wout[k % 2], wb.h[:, :], w_out[128 * k:128 * (k + 1), :],
                [], [wb])
            for f in range(NF):
                t_mm(PS[f].h[:, :], wb.h[:, 128 * f:128 * (f + 1)],
                     yq[k].h[:, :], k == 0, k == NC_ - 1, [wb, yq[k]], [PS[f]])
        for f in range(NF):
            s_cp(ast[f].h[:, :], PS[f].h[:, :], [PS[f]], [ast[f]])
        ln_fm([(ast[f], ast[f].h[:, :]) for f in range(NF)],
              lambda k: (ast[k], ast[k].h[:, :]), 512)
        for k in range(NF):
            xr = x0r[k % 2]
            dma(ch_x0r[k % 2], xr.h[:, :],
                x0_s[128 * k:128 * (k + 1), s0:s0 + 512], [x0s_t], [xr])
            v_tt(ALU.add, ast[k].h[:, :], ast[k].h[:, :], xr.h[:, :],
                 [ast[k], xr], [ast[k]])
            dma(ch_sp_x1, x1_s[128 * k:128 * (k + 1), s0:s0 + 512],
                ast[k].h[:, :], [ast[k]], [x1s_t])

    sb.phase_begin()
    masks = AT([128, 4 * 512], "masks")
    rows2 = AT([1, 1152], "rows2")
    xa = [AT([128, S], "xa%d" % k) for k in range(NF)]
    _p3mk = sb.mark()
    x1r = [AT([128, 512], "x1r%d" % k) for k in range(NF)]
    ch_x1r = [em.chan("x1r")] * NF
    gb2 = [AT([128, 512], "gb2_%d" % k) for k in range(NF)]
    ch_sp_g2 = em.chan("sp_g2")
    gs2_t = T(g_s, "g_s")
    ch_masks = em.chan("masks")
    dma(ch_masks, masks.h[:, :], masks_d[:, :], [], [masks])

    for t in range(NQ):
        w0 = 512 * t
        for k in range(NF):
            dma(ch_x1r[k], x1r[k].h[:, :],
                x1_s[128 * k:128 * (k + 1), w0:w0 + 512], [x1s_t], [x1r[k]])
        ln_fm([(x1r[k], x1r[k].h[:, :]) for k in range(NF)],
              lambda k: (gb2[k], gb2[k].h[:, :]), 512)
        for k in range(NF):
            dma(ch_sp_g2, g_s[128 * k:128 * (k + 1), w0:w0 + 512],
                gb2[k].h[:, :], [gb2[k]], [gs2_t])
        ln_fm([(gb2[k], gb2[k].h[:, :]) for k in range(NF)],
              lambda k, w0=w0: (xa[k], xa[k].h[:, w0:w0 + 512]), 512,
              scale=LN1S, shift=LN1B, par_t=vecd)

    sb.reset(_p3mk)
    qhr = [AT([65, S], "qh%d" % i) for i in range(2)]
    khr = [AT([65, S], "kh%d" % i) for i in range(2)]
    vh4 = [AT([128, NKT * 65], "vh%d" % i) for i in range(4)]
    wvb = [AT([128, 256], "wvb%d" % k) for k in range(NF)]
    ch_wvb = [em.chan("wvb")] * NF
    wqs = [AT([128, NF * HD], "wqs%d" % i) for i in range(2)]
    wks = [AT([128, NF * HD], "wks%d" % i) for i in range(2)]
    ch_wqs = [em.chan("wqs%d" % i) for i in range(2)]
    ch_wks = [em.chan("wks%d" % i) for i in range(2)]
    pt = [AT([128, 512], "pt%d" % i) for i in range(2)]
    sqh = AT([64, 512], "sqh")
    ctxh = AT([65, S], "ctxh")
    ch_sp_ctx = em.chan("sp_ctx")
    ctxs_t = T(ctx_s, "ctx_s")
    for vt in vh4:
        for kt in range(NKT):
            v_ms(vt.h[:, 65 * kt + 64:65 * kt + 65], 1.0, vt)

    mk_row = rows2.h[0:1, 0:1]
    nrm = rows2.h[0:1, 64:576]
    augr = rows2.h[0:1, 576:1088]
    ch_aug = em.chan("aug")

    ch_spi = [em.chan("spi%d" % i) for i in range(2)]
    ch_sph = em.chan("sph")
    ch_splo = em.chan("splo")
    w1h_t = T(w1h_s, "w1h_s")
    w1l_t = T(w1l_s, "w1l_s")
    w2h_t = T(w2h_s, "w2h_s")
    w2l_t = T(w2l_s, "w2l_s")
    w3h_t = T(w3h_s, "w3h_s")
    w3l_t = T(w3l_s, "w3l_s")
    spl_jobs = [(w1_r, w1h_t, w1l_t, NE * 16 * 128, 1024),
                (w2_r, w2h_t, w2l_t, NE * 16 * 128, 1024),
                (w3_2d, w3h_t, w3l_t, NE * C, 1024)]
    hi_v = [lnrb.h[:, 0:256].bitcast(BF16), lnrb.h[:, 256:512].bitcast(BF16)]
    lo_v = [lnmb.h[:, 0:256].bitcast(BF16), lnmb.h[:, 256:512].bitcast(BF16)]
    si = 0
    for srcd, hit, lot, nrows, ncols in spl_jobs:
        hid = hit.h
        lod = lot.h
        for r0 in range(0, nrows, 128):
            for c0 in range(0, ncols, 512):
                sl = si % 2
                fx = sqr_t[sl]
                dma(ch_spi[sl], fx.h[:, :], srcd[r0:r0 + 128, c0:c0 + 512],
                    [], [fx], eng="gpsimd")
                s_cp(hi_v[sl], fx.h[:, :], [fx], [lnrb])
                v_tt(ALU.subtract, lo_v[sl], fx.h[:, :], hi_v[sl],
                     [fx, lnrb], [lnmb])
                dma(ch_sph, hid[r0:r0 + 128, c0:c0 + 512], hi_v[sl],
                    [lnrb], [hit], eng="gpsimd")
                dma(ch_splo, lod[r0:r0 + 128, c0:c0 + 512], lo_v[sl],
                    [lnmb], [lot], eng="gpsimd")
                si += 1

    for hg in range(4):
        for k in range(NF):
            dma(ch_wvb[k], wvb[k].h[:, :],
                wv_d[128 * k:128 * (k + 1), 256 * hg:256 * (hg + 1)],
                [], [wvb[k]])
        for kt in range(NKT):
            ps = PS[kt % 2]
            for k in range(NF):
                t_mm(ps.h[0:128, 0:256], xa[k].h[:, 128 * kt:128 * (kt + 1)],
                     wvb[k].h[:, :], k == 0, k == NF - 1, [xa[k], wvb[k]], [ps])
            for hl in range(4):
                s_cp(vh4[hl].h[:, 65 * kt:65 * kt + 64],
                     ps.h[0:128, 64 * hl:64 * hl + 64], [ps], [vh4[hl]])
      
        for hl in range(4):
            hd_ = 4 * hg + hl
            vh = vh4[hl]
            qh = qhr[hd_ % 2]
            kh = khr[hd_ % 2]
            wq_t = wqs[hd_ % 2]
            wk_t = wks[hd_ % 2]
            dma(ch_wqs[hd_ % 2], wq_t.h[:, :],
                wq_r[128 * hd_:128 * (hd_ + 1), :], [], [wq_t])
            dma(ch_wks[hd_ % 2], wk_t.h[:, :],
                wk_r[128 * hd_:128 * (hd_ + 1), :], [], [wk_t])
            for dst, wt in ((qh, wq_t), (kh, wk_t)):
                for t in range(NQ):
                    ps = PS[2 + t % 2]
                    for k in range(NF):
                        t_mm(ps.h[0:64, :], wt.h[:, 64 * k:64 * (k + 1)],
                             xa[k].h[:, 512 * t:512 * (t + 1)], k == 0,
                             k == NF - 1, [xa[k], wt], [ps])
                    s_cp(dst.h[0:64, 512 * t:512 * (t + 1)], ps.h[0:64, :],
                         [ps], [dst])
            v_ms(kh.h[64:65, :], 1.0, kh)
            for t in range(NQ):
                ps = PS[4]
                s_act(sqh.h[:, :], kh.h[0:64, 512 * t:512 * (t + 1)],
                      AF.Square, [kh], [sqh])
                t_mm(ps.h[0:1, :], ones.h[0:64, 0:1], sqh.h[:, :], True, True,
                     [sqh, ones], [ps])
                s_act(nrm, ps.h[0:1, :], AF.Sqrt, [ps], [rows2])
                if t == 0:
                    v_max(mk_row, nrm, [rows2], [rows2])
                else:
                    v_max(rows2.h[0:1, 1:2], nrm, [rows2], [rows2])
                    v_tt(ALU.max, mk_row, mk_row, rows2.h[0:1, 1:2],
                         [rows2], [rows2])
            for t in range(NQ):
                ps = PS[4]
                s_act(sqh.h[:, :], qh.h[0:64, 512 * t:512 * (t + 1)],
                      AF.Square, [qh], [sqh])
                t_mm(ps.h[0:1, :], ones.h[0:64, 0:1], sqh.h[:, :], True, True,
                     [sqh, ones], [ps])
                s_act(nrm, ps.h[0:1, :], AF.Sqrt, [ps], [rows2])
                v_ts(augr, nrm, mk_row, -1.0, ALU.mult, ALU.mult,
                     [rows2], [rows2])
                dma(ch_aug, qh.h[64:65, 512 * t:512 * (t + 1)], augr,
                    [rows2], [qh])
            for t in range(NQ):
                psc = PS[5 + hd_ % 2] if False else PS[5]
                nkt = 4 * (t + 1)
                for kt in range(nkt):
                    pss = PS[6 + kt % 2]
                    t_mm(pss.h[:, :], kh.h[0:65, 128 * kt:128 * (kt + 1)],
                         qh.h[0:65, 512 * t:512 * (t + 1)], True, True,
                         [kh, qh], [pss])
                    pb = pt[kt % 2]
                    s_act(pb.h[:, :], pss.h[:, :], AF.Exp, [pss], [pb],
                          scale=0.125)
                    r = kt - 4 * t
                    if r >= 0:
                        v_tt(ALU.mult, pb.h[:, :], pb.h[:, :],
                             masks.h[:, 512 * r:512 * (r + 1)],
                             [pb, masks], [pb])
                    t_mm(psc.h[0:65, :], vh.h[:, 65 * kt:65 * (kt + 1)],
                         pb.h[:, :], kt == 0, kt == nkt - 1, [vh, pb], [psc])
                s_cp(ctxh.h[0:65, 512 * t:512 * (t + 1)], psc.h[0:65, :],
                     [psc], [ctxh])
                v_rec(ctxh.h[64:65, 512 * t:512 * (t + 1)],
                      ctxh.h[64:65, 512 * t:512 * (t + 1)], [ctxh], [ctxh])
                psr = PS[4]
                t_mm(psr.h[0:64, :], ones.h[64:65, 0:64],
                     ctxh.h[64:65, 512 * t:512 * (t + 1)], True, True,
                     [ctxh, ones], [psr])
                v_tt(ALU.mult, ctxh.h[0:64, 512 * t:512 * (t + 1)],
                     ctxh.h[0:64, 512 * t:512 * (t + 1)], psr.h[0:64, :],
                     [ctxh, psr], [ctxh])
            dma(ch_sp_ctx, ctx_s[64 * hd_:64 * (hd_ + 1), :], ctxh.h[0:64, :],
                [ctxh], [ctxs_t])

    sb.phase_begin()
    wob = [AT([128, D], "wob%d" % i) for i in range(2)]
    ch_wob = [em.chan("wob%d" % i) for i in range(2)]
    ctxr = [AT([128, 512], "ctxr%d" % i) for i in range(3)]
    ch_ctxr = [em.chan("ctxr%d" % i) for i in range(3)]
    gr3 = [AT([128, 512], "gr3_%d" % k) for k in range(NF)]
    ch_gr3 = [em.chan("gr3")] * NF
    ab = [AT([128, 512], "ab%d" % k) for k in range(NF)]
    xl2 = [AT([128, 512], "xl2_%d" % k) for k in range(NF)]
    xl2ht = [AT([128, 512], "xl2ht%d" % i, BF16) for i in range(2)]
    xl2lt = [AT([128, 512], "xl2lt%d" % i, BF16) for i in range(2)]
    ch_sp_a = em.chan("sp_a")
    ch_sp_xl2 = em.chan("sp_xl2")
    ch_sp_xl2l = em.chan("sp_xl2l")
    as_t = T(a_s, "a_s")
    xl2hs_t = T(xl2h_s, "xl2h_s")
    xl2ls_t = T(xl2l_s, "xl2l_s")

    for t in range(NQ):
        w0 = 512 * t
        for k in range(NF):
            cr = ctxr[k % 3]
            dma(ch_ctxr[k % 3], cr.h[:, :],
                ctx_s[128 * k:128 * (k + 1), w0:w0 + 512], [ctxs_t], [cr])
            wb = wob[k % 2]
            dma(ch_wob[k % 2], wb.h[:, :], wo[128 * k:128 * (k + 1), :],
                [], [wb])
            for f in range(NF):
                t_mm(PS[f].h[:, :], wb.h[:, 128 * f:128 * (f + 1)],
                     cr.h[:, :], k == 0, k == NF - 1, [wb, cr], [PS[f]])
        for f in range(NF):
            dma(ch_gr3[f], gr3[f].h[:, :],
                g_s[128 * f:128 * (f + 1), w0:w0 + 512], [gs2_t], [gr3[f]])
        for f in range(NF):
            v_stt(ab[f].h[:, :], PS[f].h[:, :], BO(f), gr3[f].h[:, :],
                  ALU.add, ALU.add, [PS[f], vecd, gr3[f]], [ab[f]])
            dma(ch_sp_a, a_s[128 * f:128 * (f + 1), w0:w0 + 512],
                ab[f].h[:, :], [ab[f]], [as_t])
        ln_fm([(ab[f], ab[f].h[:, :]) for f in range(NF)],
              lambda k: (xl2[k], xl2[k].h[:, :]), 512,
              scale=LN2S, shift=LN2B, par_t=vecd)
        for k in range(NF):
            ht = xl2ht[k % 2]
            lt = xl2lt[k % 2]
            s_cp(ht.h[:, :], xl2[k].h[:, :], [xl2[k]], [ht])
            v_tt(ALU.subtract, lt.h[:, :], xl2[k].h[:, :], ht.h[:, :],
                 [xl2[k], ht], [lt])
            dma(ch_sp_xl2, xl2h_s[128 * k:128 * (k + 1), w0:w0 + 512],
                ht.h[:, :], [ht], [xl2hs_t])
            dma(ch_sp_xl2l, xl2l_s[128 * k:128 * (k + 1), w0:w0 + 512],
                lt.h[:, :], [lt], [xl2ls_t])

    sb.phase_begin()
    xhh = [AT([128, SH], "xhh%d" % k, BF16) for k in range(NF)]
    xll = [AT([128, SH], "xll%d" % k, BF16) for k in range(NF)]
    ch_xhh = [em.chan("xlh")] * NF
    ch_xll = [em.chan("xlh")] * NF
    heh = [AT([128, SH], "heh%d" % m, BF16) for m in range(8)]
    hel = [AT([128, SH], "hel%d" % m, BF16) for m in range(8)]
    hf = [AT([128, 512], "hf0")] * 2
    acc = [AT([128, SH], "acc%d" % k) for k in range(NF)]
    w1b = [AT([128, 8 * 128], "w1b%d" % i, BF16) for i in range(4)]
    w2b = [AT([128, 8 * 128], "w2b%d" % i, BF16) for i in range(4)]
    _w12a, _w12b = em.chan("w12a"), em.chan("w12b")
    ch_w1b = [_w12a, _w12a, _w12b, _w12b]
    ch_w2b = [_w12a, _w12a, _w12b, _w12b]
    w3hb = [AT([128, 512], "w3hb%d" % i, BF16) for i in range(3)]
    w3lb = [AT([128, 512], "w3lb%d" % i, BF16) for i in range(3)]
    ch_w3 = [em.chan("w3c%d" % i) for i in range(3)]
    silu_b = AT([128, 512], "silu_b")
    tmpb = AT([128, 512], "tmpb")
    wtm = AT([128, 64], "wtm")
    wte = AT([8, SH], "wte")
    wtes_t = T(wte_s, "wte_s")
    ch_sp_wte = em.chan("sp_wte")
    routeb = AT([128, SH], "routeb")
    ch_rb = em.chan("rb")
    gsc = AT([8, 512], "gsc")
    r8 = AT([128, 8], "r8")
    r8b = AT([128, 8], "r8b")
    rcol = AT([128, 4], "rcol")
    ar2 = [AT([128, 512], "ar2_%d" % k) for k in range(NF)]
    ch_ar2 = [em.chan("ar2")] * NF
    x1r2 = [AT([128, 512], "x1r2_%d" % i) for i in range(2)]
    ch_x1r2 = [em.chan("x1r2%d" % i) for i in range(2)]
    outst = AT([128, 512], "outst")
    ch_out = em.chan("outc")
    out_t = T(out, "out")

    NTH = SH // 512
    for sh in range(2):
        s0 = sh * SH
        for k in range(NF):
            dma(ch_xhh[k], xhh[k].h[:, :],
                xl2h_s[128 * k:128 * (k + 1), s0:s0 + SH], [xl2hs_t], [xhh[k]])
            dma(ch_xll[k], xll[k].h[:, :],
                xl2l_s[128 * k:128 * (k + 1), s0:s0 + SH], [xl2ls_t], [xll[k]])
            v_ms(acc[k].h[:, :], 0.0, acc[k])
        for t in range(NTH):
            ps = PS[0]
            for k in range(NF):
                mm3(ps.h[0:8, :], gwh.h[:, 8 * k:8 * (k + 1)],
                    gwl.h[:, 8 * k:8 * (k + 1)],
                    xhh[k].h[:, 512 * t:512 * (t + 1)],
                    xll[k].h[:, 512 * t:512 * (t + 1)], k == 0, k == NF - 1,
                    [gwh, gwl, xhh[k], xll[k]], [ps])
            s_cp(gsc.h[0:8, :], ps.h[0:8, :], [ps], [gsc])
            for si in range(4):
                sc = 4 * t + si
                ps2 = PS[1]
                t_tr(ps2.h[0:128, 0:8], gsc.h[0:8, 128 * si:128 * (si + 1)],
                     [gsc], [ps2], idw=8)
                s_cp(r8.h[:, :], ps2.h[0:128, 0:8], [ps2], [r8])
                v_max(rcol.h[:, 0:1], r8.h[:, :], [r8], [rcol])
                v_ts(r8b.h[:, :], r8.h[:, :], rcol.h[:, 0:1], -1e30,
                     ALU.is_equal, ALU.mult, [r8, rcol], [r8b])
                v_tt(ALU.add, r8b.h[:, :], r8.h[:, :], r8b.h[:, :],
                     [r8, r8b], [r8b])
                v_max(rcol.h[:, 1:2], r8b.h[:, :], [r8b], [rcol])
                v_tt(ALU.subtract, rcol.h[:, 2:3], rcol.h[:, 1:2],
                     rcol.h[:, 0:1], [rcol], [rcol])
                s_act(rcol.h[:, 2:3], rcol.h[:, 2:3], AF.Exp, [rcol], [rcol])
                v_ts(rcol.h[:, 3:4], rcol.h[:, 2:3], 1.0, None, ALU.add,
                     ALU.bypass, [rcol], [rcol])
                v_rec(rcol.h[:, 3:4], rcol.h[:, 3:4], [rcol], [rcol])
                v_tt(ALU.mult, rcol.h[:, 2:3], rcol.h[:, 2:3],
                     rcol.h[:, 3:4], [rcol], [rcol])
                v_ts(wtm.h[:, 0:8], r8.h[:, :], rcol.h[:, 0:1],
                     rcol.h[:, 3:4], ALU.is_equal, ALU.mult,
                     [r8, rcol], [wtm])
                v_ts(r8b.h[:, :], r8b.h[:, :], rcol.h[:, 1:2],
                     rcol.h[:, 2:3], ALU.is_equal, ALU.mult,
                     [r8b, rcol], [r8b])
                v_tt(ALU.add, wtm.h[:, 0:8], wtm.h[:, 0:8], r8b.h[:, :],
                     [wtm, r8b], [wtm])
                ps3 = PS[2]
                t_tr(ps3.h[0:8, 0:128], wtm.h[:, 0:8], [wtm], [ps3])
                s_cp(wte.h[0:8, 128 * sc:128 * (sc + 1)], ps3.h[0:8, 0:128],
                     [ps3], [wte])
        dma(ch_sp_wte, wte_s[0:8, s0:s0 + SH], wte.h[0:8, :], [wte], [wtes_t])
        for e in range(NE):
            dma(ch_rb, routeb.h[:, :],
                wte_s[e:e + 1, s0:s0 + SH].partition_broadcast(128),
                [wtes_t], [routeb])
            for mh in range(2):
                for mi in range(8):
                    m = 8 * mh + mi
                    b1h = w1b[2 * (mi % 2)]
                    b1l = w1b[2 * (mi % 2) + 1]
                    b2h = w2b[2 * (mi % 2)]
                    b2l = w2b[2 * (mi % 2) + 1]
                    row0 = (e * 16 + m) * 128
                    dma(ch_w1b[2 * (mi % 2)], b1h.h[:, :],
                        w1h_s[row0:row0 + 128, :], [w1h_t], [b1h])
                    dma(ch_w1b[2 * (mi % 2) + 1], b1l.h[:, :],
                        w1l_s[row0:row0 + 128, :], [w1l_t], [b1l])
                    dma(ch_w2b[2 * (mi % 2)], b2h.h[:, :],
                        w2h_s[row0:row0 + 128, :], [w2h_t], [b2h])
                    dma(ch_w2b[2 * (mi % 2) + 1], b2l.h[:, :],
                        w2l_s[row0:row0 + 128, :], [w2l_t], [b2l])
                    for t in range(NTH):
                        p1 = PS[0 + t % 2]
                        p2 = PS[2 + t % 2]
                        for k in range(NF):
                            mm3(p1.h[:, :], b1h.h[:, 128 * k:128 * (k + 1)],
                                b1l.h[:, 128 * k:128 * (k + 1)],
                                xhh[k].h[:, 512 * t:512 * (t + 1)],
                                xll[k].h[:, 512 * t:512 * (t + 1)],
                                k == 0, k == NF - 1,
                                [b1h, b1l, xhh[k], xll[k]], [p1])
                        for k in range(NF):
                            mm3(p2.h[:, :], b2h.h[:, 128 * k:128 * (k + 1)],
                                b2l.h[:, 128 * k:128 * (k + 1)],
                                xhh[k].h[:, 512 * t:512 * (t + 1)],
                                xll[k].h[:, 512 * t:512 * (t + 1)],
                                k == 0, k == NF - 1,
                                [b2h, b2l, xhh[k], xll[k]], [p2])
                        s_act(silu_b.h[:, :], p1.h[:, :], AF.Silu,
                              [p1], [silu_b])
                        hft = hf[t % 2]
                        v_tt(ALU.mult, hft.h[:, :],
                             silu_b.h[:, :], p2.h[:, :], [silu_b, p2], [hft])
                        s_cp(heh[mi].h[:, 512 * t:512 * (t + 1)], hft.h[:, :],
                             [hft], [heh[mi]])
                        v_tt(ALU.subtract,
                             hel[mi].h[:, 512 * t:512 * (t + 1)], hft.h[:, :],
                             heh[mi].h[:, 512 * t:512 * (t + 1)],
                             [hft, heh[mi]], [hel[mi]])
                for fh in range(2):
                    for ki in range(8):
                        k = 8 * mh + ki
                        wbh = w3hb[ki % 3]
                        wbl = w3lb[ki % 3]
                        dma(ch_w3[ki % 3], wbh.h[:, :],
                            w3h_s[e * C + 128 * k:e * C + 128 * (k + 1),
                                  512 * fh:512 * (fh + 1)], [w3h_t], [wbh])
                        dma(ch_w3[ki % 3], wbl.h[:, :],
                            w3l_s[e * C + 128 * k:e * C + 128 * (k + 1),
                                  512 * fh:512 * (fh + 1)], [w3l_t], [wbl])
                        for t in range(NTH):
                            for f in range(4):
                                mm3(PS[4 * t + f].h[:, :],
                                    wbh.h[:, 128 * f:128 * (f + 1)],
                                    wbl.h[:, 128 * f:128 * (f + 1)],
                                    heh[ki].h[:, 512 * t:512 * (t + 1)],
                                    hel[ki].h[:, 512 * t:512 * (t + 1)],
                                    ki == 0, ki == 7,
                                    [wbh, wbl, heh[ki], hel[ki]],
                                    [PS[4 * t + f]])
                    for t in range(NTH):
                        for f in range(4):
                            fi = 4 * fh + f
                            v_tt(ALU.mult, tmpb.h[:, :],
                                 PS[4 * t + f].h[:, :],
                                 routeb.h[:, 512 * t:512 * (t + 1)],
                                 [PS[4 * t + f], routeb], [tmpb])
                            v_tt(ALU.add,
                                 acc[fi].h[:, 512 * t:512 * (t + 1)],
                                 acc[fi].h[:, 512 * t:512 * (t + 1)],
                                 tmpb.h[:, :], [acc[fi], tmpb], [acc[fi]])
        for t in range(NTH):
            w0 = s0 + 512 * t
            for k in range(NF):
                dma(ch_ar2[k], ar2[k].h[:, :],
                    a_s[128 * k:128 * (k + 1), w0:w0 + 512], [as_t], [ar2[k]])
            for k in range(NF):
                v_tt(ALU.add, ar2[k].h[:, :], ar2[k].h[:, :],
                     acc[k].h[:, 512 * t:512 * (t + 1)],
                     [ar2[k], acc[k]], [ar2[k]])
            ln_fm([(ar2[k], ar2[k].h[:, :]) for k in range(NF)],
                  lambda k: (ar2[k], ar2[k].h[:, :]), 512)
            for k in range(NF):
                xr = x1r2[k % 2]
                dma(ch_x1r2[k % 2], xr.h[:, :],
                    x1_s[128 * k:128 * (k + 1), w0:w0 + 512], [x1s_t], [xr])
                v_tt(ALU.add, ar2[k].h[:, :], ar2[k].h[:, :], xr.h[:, :],
                     [ar2[k], xr], [ar2[k]])
            for sc in range(4):
                for kh2 in range(2):
                    for kk in range(4):
                        k = 4 * kh2 + kk
                        ps = PS[k % 2]
                        t_tr(ps.h[:, 0:128],
                             ar2[k].h[:, 128 * sc:128 * (sc + 1)],
                             [ar2[k]], [ps])
                        s_cp(outst.h[:, 128 * kk:128 * (kk + 1)],
                             ps.h[:, 0:128], [ps], [outst])
                    dma(ch_out,
                        out[w0 + 128 * sc:w0 + 128 * (sc + 1),
                            512 * kh2:512 * (kh2 + 1)],
                        outst.h[:, :], [outst], [out_t])

    em.emit("sync", None, reads=[out_t], writes=[])
    em.finalize()
    es.close()
    return nc


def _prep(inputs):
    ct = np.ascontiguousarray
    f32 = lambda a: np.asarray(a, dtype=np.float32)

    def head_repack(w):
        return ct(w.reshape(NF, 128, NH, HD).transpose(2, 1, 0, 3)
                  .reshape(NH * 128, NF * HD))

    def w12_repack(w):
        return ct(w.reshape(NE, NF, 128, NC_, 128).transpose(0, 3, 2, 1, 4)
                  .reshape(NE * NC_ * 128, NF * 128))

    def vchunks(v, n):
        return ct(v.reshape(n, 128).T)

    conv_w = f32(inputs["L0_attn_conv_w"])
    vecs = np.zeros((128, 48), np.float32)
    vecs[:, 0:16] = vchunks(f32(inputs["L0_attn_conv_b"]), 16)
    vecs[:, 16:32] = vchunks(f32(inputs["L0_attn_b_dt"]), 16)
    vecs[:, 32:48] = vchunks(f32(inputs["L0_attn_D"]), 16)
    vecd = np.zeros((128, 40), np.float32)
    vecd[:, 0:8] = vchunks(f32(inputs["L0_ffn_ln1_scale"]), 8)
    vecd[:, 8:16] = vchunks(f32(inputs["L0_ffn_ln1_shift"]), 8)
    vecd[:, 16:24] = vchunks(f32(inputs["L0_ffn_ln2_scale"]), 8)
    vecd[:, 24:32] = vchunks(f32(inputs["L0_ffn_ln2_shift"]), 8)
    vecd[:, 32:40] = vchunks(f32(inputs["L0_ffn_bo"]), 8)
    cw_r = ct(conv_w[:, 0, :].T.reshape(16, 128, 4).transpose(1, 0, 2)
              .reshape(128, 64))
    alog_r = ct(f32(inputs["L0_attn_A_log"]).reshape(16, 128, 16)
                .transpose(1, 0, 2).reshape(128, 256))
    gw_r = ct(f32(inputs["L0_ffn_gate_w"]).reshape(8, 128, 8)
              .transpose(1, 0, 2).reshape(128, 64))
    masks = np.zeros((128, 4 * 512), np.float32)
    i = np.arange(128)[:, None]
    j = np.arange(512)[None, :]
    for r in range(4):
        masks[:, 512 * r:512 * (r + 1)] = (j - 128 * r >= i)

    return {
        "w_in": f32(inputs["L0_attn_W_in"]),
        "w_x": f32(inputs["L0_attn_W_x"]),
        "w_dt": f32(inputs["L0_attn_W_dt"]),
        "w_out": f32(inputs["L0_attn_W_out"]),
        "wq_r": head_repack(f32(inputs["L0_ffn_wq"])),
        "wk_r": head_repack(f32(inputs["L0_ffn_wk"])),
        "wv": f32(inputs["L0_ffn_wv"]),
        "wo": f32(inputs["L0_ffn_wo"]),
        "w1_r": w12_repack(f32(inputs["L0_ffn_w1"])),
        "w2_r": w12_repack(f32(inputs["L0_ffn_w2"])),
        "w3": ct(f32(inputs["L0_ffn_w3"]).reshape(NE * C, D)),
        "vecs": vecs, "vecd": vecd, "cw_r": cw_r, "alog_r": alog_r,
        "gw_r": gw_r, "ident": np.eye(128, dtype=np.float32), "masks": masks,
    }


def kernel(**inputs):
    S = 2048
    x = np.asarray(inputs["L0_attn_x"], dtype=np.float32)
    B = x.shape[0]
    shared = _prep(inputs)
    nc = build_program(S)
    in_maps = [dict(shared, x=np.ascontiguousarray(x[i])) for i in range(B)]
    res = run_bass_kernel_spmd(nc, in_maps, core_ids=list(range(B)))
    return np.stack([np.asarray(r["out"]) for r in res.results], axis=0)


# version 07
# speedup vs baseline: 1.9437x; 1.2270x over previous
import numpy as np
from contextlib import ExitStack

import concourse.bass as bass
import concourse.mybir as mybir
from concourse.bass_utils import run_bass_kernel_spmd

F32 = mybir.dt.float32
BF16 = mybir.dt.bfloat16
U8 = mybir.dt.uint8
AF = mybir.ActivationFunctionType
ALU = mybir.AluOpType
AX = mybir.AxisListType

ENGS = ("sync", "tensor", "scalar", "vector", "gpsimd")
PROG_ENGS = ("tensor", "scalar", "vector", "gpsimd")

D = 1024
C = 2048
NF = D // 128
NC_ = C // 128
NJ = 16
NH = 16
HD = 64
NE = 8
EPS = 1e-5


class Chan:
    __slots__ = ("sem", "n", "name")

    def __init__(self, sem, name):
        self.sem = sem
        self.n = 0
        self.name = name


TILE_NAMES = {}


class Tile:
    __slots__ = ("h", "name", "w", "rd")

    def __init__(self, h, name):
        self.h = h
        self.name = name
        self.w = None
        self.rd = {}
        try:
            TILE_NAMES[h.name] = name
        except Exception:
            pass


class Em:
    def __init__(self, nc, es):
        self.nc = nc
        self.es = es
        self.q = {e: [] for e in ENGS}
        self.idx = {e: 0 for e in ENGS}
        self.need = {e: set() for e in ENGS}
        self.prog = {e: es.enter_context(nc.semaphore("prog_" + e))
                     for e in PROG_ENGS}
        self.chans = []
        self.pending = {}

    def chan(self, name):
        if not hasattr(self, "_chan_cache"):
            self._chan_cache = {}
        if name in self._chan_cache:
            return self._chan_cache[name]
        ch = Chan(self.es.enter_context(self.nc.semaphore("ch_" + name)), name)
        self.chans.append(ch)
        self._chan_cache[name] = ch
        return ch

    def _rdep(self, t):
        k, a, v = t.w
        if k == "d":
            return ("d", a, a.n)
        return t.w

    def emit(self, eng, build, reads=(), writes=(), dma=None):
        deps = []
        for t in reads:
            if t.w is not None:
                deps.append(self._rdep(t))
        for t in writes:
            if t.w is not None:
                deps.append(self._rdep(t))
            deps.extend(t.rd.values())
        if self.pending.get(eng):
            deps.extend(self.pending[eng])
            self.pending[eng] = None
        best = {}
        for d in deps:
            if d[0] == "e":
                if d[1] == eng and eng == "tensor":
                    continue
                key = ("e", d[1])
            else:
                key = ("d", id(d[1]))
            if key not in best or d[2] > best[key][2]:
                best[key] = d
        fdeps = list(best.values())
        for d in fdeps:
            if d[0] == "e":
                self.need[d[1]].add(d[2])
        if dma is not None:
            dma.n += 16
            tag = ("d", dma, dma.n)
            self.q[eng].append((fdeps, build, dma, None))
        else:
            self.idx[eng] += 1
            tag = ("e", eng, self.idx[eng])
            self.q[eng].append((fdeps, build, None, self.idx[eng]))
        for t in writes:
            t.w = tag
            t.rd = {}
        rkey = ("d", id(dma)) if dma is not None else ("e", eng)
        for t in reads:
            t.rd[rkey] = tag
        return tag

    def barrier(self):
        deps = [("e", e, self.idx[e]) for e in PROG_ENGS if self.idx[e] > 0]
        deps += [("d", ch, ch.n) for ch in self.chans if ch.n > 0]
        for e in ENGS:
            self.pending[e] = list(deps)
        for d in deps:
            if d[0] == "e":
                self.need[d[1]].add(d[2])

    def finalize(self):
        rank = {e: {i: r + 1 for r, i in enumerate(sorted(self.need[e]))}
                for e in PROG_ENGS}

        def runner(eng):
            def run(e):
                known = {}
                for fdeps, build, dma, myidx in self.q[eng]:
                    for d in fdeps:
                        if d[0] == "e":
                            tgt = rank[d[1]][d[2]]
                            sem = self.prog[d[1]]
                            key = ("e", d[1])
                        else:
                            tgt = d[2]
                            sem = d[1].sem
                            key = ("d", id(d[1]))
                        if known.get(key, 0) < tgt:
                            e.wait_ge(sem, tgt)
                            known[key] = tgt
                    if build is None:
                        continue
                    ins = build(e)
                    if dma is not None:
                        ins.then_inc(dma.sem, 16)
                    elif myidx in self.need[eng]:
                        ins.then_inc(self.prog[eng], 1)
            return run

        with self.nc.Block() as block:
            for eng in ENGS:
                getattr(block, eng)(runner(eng))


class SB:
    def __init__(self, nc, total):
        self.nc = nc
        self.h = nc.alloc_sbuf_tensor("arena", [128, total], U8)
        self.base = nc.lookup_mloc(self.h).addr
        self.total = total
        self.off = 0
        self.ph_base = None
        self.ph_off = 0
        self.cnt = 0

    def _mk(self, shape, dtype, off):
        nbytes = int(np.prod(shape[1:])) * mybir.dt.size(dtype)
        assert off + nbytes <= self.total, \
            ("sbuf overflow", off, nbytes, self.total)
        self.cnt += 1
        return self.nc.alloc_sbuf_tensor_at(
            "sb%d" % self.cnt, list(shape), dtype, offset=self.base + off)

    def persist(self, shape, dtype=F32):
        assert self.ph_base is None
        off = (self.off + 31) & ~31
        h = self._mk(shape, dtype, off)
        self.off = off + int(np.prod(shape[1:])) * mybir.dt.size(dtype)
        return h

    def phase_begin(self):
        if self.ph_base is None:
            self.ph_base = (self.off + 127) & ~127
        self.ph_off = self.ph_base

    def mark(self):
        return self.ph_off

    def reset(self, mk):
        self.ph_off = mk

    def alloc(self, shape, dtype=F32):
        off = (self.ph_off + 31) & ~31
        h = self._mk(shape, dtype, off)
        self.ph_off = off + int(np.prod(shape[1:])) * mybir.dt.size(dtype)
        self.last_rng = (off, self.ph_off)
        return h


def build_program(S):
    NQ = S // 512
    SQ = 512
    SH = S // 2
    NKT = S // 128

    nc = bass.Bass("TRN2", target_bir_lowering=False, debug=False)
    es = ExitStack()
    em = Em(nc, es)

    def dram(name, shape, kind="Internal"):
        return nc.dram_tensor(name, list(shape), F32, kind=kind)

    x_in = dram("x", (S, D), "ExternalInput")
    out = dram("out", (S, D), "ExternalOutput")
    w_in = dram("w_in", (D, 2 * C), "ExternalInput")
    w_x = dram("w_x", (C, 96), "ExternalInput")
    w_dt = dram("w_dt", (64, C), "ExternalInput")
    w_out = dram("w_out", (C, D), "ExternalInput")
    wq_r = dram("wq_r", (NH * 128, NF * HD), "ExternalInput")
    wk_r = dram("wk_r", (NH * 128, NF * HD), "ExternalInput")
    wv_d = dram("wv", (D, D), "ExternalInput")
    wo = dram("wo", (D, D), "ExternalInput")
    w1_r = dram("w1_r", (NE * 16 * 128, 8 * 128), "ExternalInput")
    w2_r = dram("w2_r", (NE * 16 * 128, 8 * 128), "ExternalInput")
    w3_2d = dram("w3", (NE * C, D), "ExternalInput")
    vecs = dram("vecs", (128, 48), "ExternalInput")
    vecd_d = dram("vecd", (128, 40), "ExternalInput")
    cw_d = dram("cw_r", (128, 64), "ExternalInput")
    alog_d = dram("alog_r", (128, 16 * NC_), "ExternalInput")
    gw_d = dram("gw_r", (128, 64), "ExternalInput")
    ident_d = dram("ident", (128, 128), "ExternalInput")
    masks_d = dram("masks", (128, 4 * 512), "ExternalInput")
    h_s = dram("h_s", (C, S))
    gate_s = dram("gate_s", (C, S))
    x0_s = dram("x0_s", (D, S))
    x1_s = dram("x1_s", (D, S))
    g_s = dram("g_s", (D, S))
    a_s = dram("a_s", (D, S))
    xln2_s = dram("xln2_s", (D, S))
    ctx_s = dram("ctx_s", (D, S))
    bc_s = dram("bc_s", (32, S))
    wte_s = dram("wte_s", (8, S))
    dramb = lambda name, shape: nc.dram_tensor(name, list(shape), BF16,
                                               kind="Internal")
    w1h_s = dramb("w1h_s", (NE * 16 * 128, 8 * 128))
    w1l_s = dramb("w1l_s", (NE * 16 * 128, 8 * 128))
    w2h_s = dramb("w2h_s", (NE * 16 * 128, 8 * 128))
    w2l_s = dramb("w2l_s", (NE * 16 * 128, 8 * 128))
    w3h_s = dramb("w3h_s", (NE * C, D))
    w3l_s = dramb("w3l_s", (NE * C, D))
    xl2h_s = dramb("xl2h_s", (D, S))
    xl2l_s = dramb("xl2l_s", (D, S))

    rem = nc.sbuf_bytes_remaining
    sb = SB(nc, rem - 4096)

    T = Tile
    ident = T(sb.persist([128, 128]), "ident")
    ones = T(sb.persist([128, 128]), "ones")
    cw = T(sb.persist([128, 64]), "cw")
    vec = T(sb.persist([128, 48]), "vec")
    vecd = T(sb.persist([128, 40]), "vecd")
    apb = T(sb.persist([128, 16 * NC_]), "apb")
    gw = T(sb.persist([128, 64]), "gw")
    gwh = T(sb.persist([128, 64], BF16), "gwh")
    gwl = T(sb.persist([128, 64], BF16), "gwl")
    ssm = T(sb.persist([96, S]), "ssm")
    carry = T(sb.persist([128, 16 * NC_]), "carry")
    wdt = T(sb.persist([64, C]), "wdt")
    wx = T(sb.persist([128, 96 * NC_]), "wx")
    halo = T(sb.persist([128, 3 * NC_]), "halo")
    rows = T(sb.persist([1, 3072]), "rows")
    sqr_t = [T(sb.persist([128, 512]), "lnsq%d" % i) for i in range(2)]
    lnrb = T(sb.persist([128, 512]), "lnrb")
    lnmb = T(sb.persist([128, 512]), "lnmb")

    PS = [T(nc.alloc_psum_tensor("ps%d" % i, [128, 512], F32), "ps%d" % i)
          for i in range(8)]

    alias_live = []

    def AT(shape, name, dtype=F32):
        h = sb.alloc(shape, dtype)
        lo, hi = sb.last_rng
        t = Tile(h, name)
        for (a, b, pt) in alias_live:
            if a < hi and lo < b and pt is not t:
                if pt.w is not None:
                    t.rd[("al", id(pt), "w")] = pt.w
                for k, v in pt.rd.items():
                    t.rd[("al", id(pt), k)] = v
        alias_live.append((lo, hi, t))
        return t

    def t_mm(o_ap, l_ap, r_ap, st, sp, rd, wr):
        em.emit("tensor",
                lambda e, o=o_ap, l=l_ap, r=r_ap, a=st, b=sp:
                e.matmul(o, l, r, start=a, stop=b), rd, wr)

    def t_tr(o_ap, i_ap, rd, wr, idw=128):
        em.emit("tensor",
                lambda e, o=o_ap, i=i_ap, d=ident.h[0:idw, 0:idw]:
                e.transpose(o, i, d), list(rd) + [ident], wr)

    def mm3(ps_ap, lh, ll, rh, rl, start, stop, rd, wr):
        t_mm(ps_ap, lh, rh, start, False, rd, wr)
        t_mm(ps_ap, lh, rl, False, False, rd, wr)
        t_mm(ps_ap, ll, rh, False, stop, rd, wr)

    def s_act(o_ap, i_ap, fn, rd, wr, bias=0.0, scale=1.0):
        em.emit("scalar",
                lambda e, o=o_ap, i=i_ap, f=fn, b=bias, s=scale:
                e.activation(o, i, f, bias=b, scale=s), rd, wr)

    def s_cp(o_ap, i_ap, rd, wr):
        em.emit("scalar", lambda e, o=o_ap, i=i_ap: e.copy(o, i), rd, wr)

    def v_tt(op, o_ap, a_ap, b_ap, rd, wr, eng="vector"):
        em.emit(eng, lambda e, o=o_ap, a=a_ap, b=b_ap, p=op:
                e.tensor_tensor(o, a, b, op=p), rd, wr)

    def v_ts(o_ap, i_ap, s1, s2, op0, op1, rd, wr, eng="vector"):
        em.emit(eng, lambda e, o=o_ap, i=i_ap, a=s1, b=s2, p0=op0, p1=op1:
                e.tensor_scalar(o, i, a, b, op0=p0, op1=p1), rd, wr)

    def v_stt(o_ap, i0_ap, sc, i1_ap, op0, op1, rd, wr, eng="vector"):
        em.emit(eng, lambda e, o=o_ap, i0=i0_ap, s=sc, i1=i1_ap, p0=op0, p1=op1:
                e.scalar_tensor_tensor(o, i0, s, i1, op0=p0, op1=p1), rd, wr)

    def v_scan(o_ap, d0_ap, d1_ap, init, rd, wr):
        em.emit("vector", lambda e, o=o_ap, a=d0_ap, b=d1_ap, i=init:
                e.tensor_tensor_scan(o, a, b, i, op0=ALU.mult, op1=ALU.add),
                rd, wr)

    def v_cp(o_ap, i_ap, rd, wr, eng="vector"):
        em.emit(eng, lambda e, o=o_ap, i=i_ap: e.tensor_copy(o, i), rd, wr)

    def v_max(o_ap, i_ap, rd, wr):
        em.emit("vector", lambda e, o=o_ap, i=i_ap:
                e.reduce_max(o, i, axis=AX.X), rd, wr)

    def v_rec(o_ap, i_ap, rd, wr):
        em.emit("vector", lambda e, o=o_ap, i=i_ap: e.reciprocal(o, i), rd, wr)

    def v_ms(ap, val, t, eng="vector"):
        em.emit(eng, lambda e, o=ap, v=val: e.memset(o, v), [], [t])

    def dma(ch, o_ap, i_ap, rd, wr, eng="sync"):
        em.emit(eng, lambda e, o=o_ap, i=i_ap: e.dma_start(out=o, in_=i),
                rd, wr, dma=ch)

    def ln_fm(srcs, dst_fn, W, scale=None, shift=None, par_t=None, res=None,
              psA=None, psB=None):
        psA = psA or PS[6]
        psB = psB or PS[7]
        nk = len(srcs)
        for k, (t, ap) in enumerate(srcs):
            t_mm(psA.h[0:1, 0:W], ones.h[:, 0:1], ap, k == 0, k == nk - 1,
                 [t, ones], [psA])
        for k, (t, ap) in enumerate(srcs):
            sq = sqr_t[k % 2]
            s_act(sq.h[:, 0:W], ap, AF.Square, [t], [sq])
            t_mm(psB.h[0:1, 0:W], ones.h[:, 0:1], sq.h[:, 0:W], k == 0,
                 k == nk - 1, [sq, ones], [psB])
        mu = rows.h[0:1, 0:W]
        m2 = rows.h[0:1, 512:512 + W]
        veps = rows.h[0:1, 1024:1024 + W]
        rstd = rows.h[0:1, 1536:1536 + W]
        mur = rows.h[0:1, 2048:2048 + W]
        tmp = rows.h[0:1, 2560:2560 + W]
        inv = 1.0 / D
        v_ts(mu, psA.h[0:1, 0:W], inv, None, ALU.mult, ALU.bypass, [psA], [rows])
        v_ts(m2, psB.h[0:1, 0:W], inv, None, ALU.mult, ALU.bypass, [psB], [rows])
        v_tt(ALU.mult, tmp, mu, mu, [rows], [rows])
        v_tt(ALU.subtract, tmp, m2, tmp, [rows], [rows])
        v_ts(veps, tmp, EPS, None, ALU.add, ALU.bypass, [rows], [rows])
        s_act(rstd, veps, AF.Sqrt, [rows], [rows])
        v_rec(rstd, rstd, [rows], [rows])
        v_tt(ALU.mult, tmp, rstd, rstd, [rows], [rows])
        v_tt(ALU.mult, tmp, tmp, veps, [rows], [rows])
        v_ts(tmp, tmp, -0.5, 1.5, ALU.mult, ALU.add, [rows], [rows])
        v_tt(ALU.mult, rstd, rstd, tmp, [rows], [rows])
        v_tt(ALU.mult, mur, mu, rstd, [rows], [rows])
        t_mm(psA.h[:, 0:W], ones.h[0:1, :], rstd, True, True, [rows, ones], [psA])
        t_mm(psB.h[:, 0:W], ones.h[0:1, :], mur, True, True, [rows, ones], [psB])
        s_cp(lnrb.h[:, 0:W], psA.h[:, 0:W], [psA], [lnrb])
        s_cp(lnmb.h[:, 0:W], psB.h[:, 0:W], [psB], [lnmb])
        for k, (t, ap) in enumerate(srcs):
            dt_, dap = dst_fn(k)
            v_tt(ALU.mult, dap, ap, lnrb.h[:, 0:W], [t, lnrb], [dt_])
            v_tt(ALU.subtract, dap, dap, lnmb.h[:, 0:W], [dt_, lnmb], [dt_])
            if scale is not None:
                v_ts(dap, dap, scale(k), shift(k), ALU.mult, ALU.add,
                     [dt_, par_t], [dt_])
            if res is not None:
                rt, rap = res(k)
                v_tt(ALU.add, dap, dap, rap, [dt_, rt], [dt_])

    ch_const = em.chan("const")
    dma(ch_const, ident.h[:, :], ident_d[:, :], [], [ident])
    dma(ch_const, cw.h[:, :], cw_d[:, :], [], [cw])
    dma(ch_const, vec.h[:, :], vecs[:, :], [], [vec])
    dma(ch_const, vecd.h[:, :], vecd_d[:, :], [], [vecd])
    dma(ch_const, apb.h[:, :], alog_d[:, :], [], [apb])
    dma(ch_const, gw.h[:, :], gw_d[:, :], [], [gw])
    dma(ch_const, wdt.h[:, :], w_dt[:, :], [], [wdt])
    for c in range(NC_):
        dma(ch_const, wx.h[:, 96 * c:96 * (c + 1)],
            w_x[128 * c:128 * (c + 1), :], [], [wx])
    v_ms(ones.h[:, :], 1.0, ones)
    s_cp(gwh.h[:, :], gw.h[:, :], [gw], [gwh])
    v_tt(ALU.subtract, gwl.h[:, :], gw.h[:, :], gwh.h[:, :], [gw, gwh], [gwl])
    s_act(apb.h[:, :], apb.h[:, :], AF.Exp, [apb], [apb])
    v_ts(apb.h[:, :], apb.h[:, :], -1.0, None, ALU.mult, ALU.bypass,
         [apb], [apb])
    for c in range(NC_):
        v_ms(halo.h[:, 3 * c:3 * (c + 1)], 0.0, halo)

    CB = lambda c: vec.h[:, c:c + 1]
    BDT = lambda c: vec.h[:, 16 + c:17 + c]
    DV = lambda c: vec.h[:, 32 + c:33 + c]
    LN1S = lambda k: vecd.h[:, k:k + 1]
    LN1B = lambda k: vecd.h[:, 8 + k:9 + k]
    LN2S = lambda k: vecd.h[:, 16 + k:17 + k]
    LN2B = lambda k: vecd.h[:, 24 + k:25 + k]
    BO = lambda k: vecd.h[:, 32 + k:33 + k]

    sb.phase_begin()
    x_tm = [AT([128, D], "x_tm%d" % i) for i in range(2)]
    ch_xtm = [em.chan("xtm%d" % i) for i in range(2)]
    x0q = [AT([128, SQ], "x0q%d" % k) for k in range(NF)]
    xlnq = [AT([128, SQ], "xlnq%d" % k) for k in range(NF)]
    winh = [AT([128, C], "winh%d" % k) for k in range(NF)]
    ch_win = [em.chan("win")] * NF
    rawx = [AT([128, 515], "rawx%d" % i) for i in range(2)]
    hbuf = [AT([128, 512], "hbuf%d" % i) for i in range(2)]
    gbuf = [AT([128, 512], "gbuf%d" % i) for i in range(2)]
    ch_sp_h = em.chan("sp_h")
    ch_sp_g = em.chan("sp_g")
    ch_sp_x0 = em.chan("sp_x0")
    hs_t = T(h_s, "h_s")
    gs_t = T(gate_s, "gate_s")
    x0s_t = T(x0_s, "x0_s")

    NSQ = S // SQ
    NT = SQ // 512
    for q in range(NSQ):
        for scn in range(SQ // 128):
            xt = x_tm[scn % 2]
            s0 = q * SQ + scn * 128
            dma(ch_xtm[scn % 2], xt.h[:, :], x_in[s0:s0 + 128, :], [], [xt])
            for k in range(NF):
                ps = PS[k % 2]
                t_tr(ps.h[:, 0:128], xt.h[:, 128 * k:128 * (k + 1)], [xt], [ps])
                s_cp(x0q[k].h[:, 128 * scn:128 * (scn + 1)], ps.h[:, 0:128],
                     [ps], [x0q[k]])
        for t in range(NT):
            w0 = 512 * t
            ln_fm([(x0q[k], x0q[k].h[:, w0:w0 + 512]) for k in range(NF)],
                  lambda k, w0=w0: (xlnq[k], xlnq[k].h[:, w0:w0 + 512]), 512)
        for k in range(NF):
            dma(ch_sp_x0, x0_s[128 * k:128 * (k + 1), q * SQ:(q + 1) * SQ],
                x0q[k].h[:, :], [x0q[k]], [x0s_t])
        for half in range(2):
            for k in range(NF):
                dma(ch_win[k], winh[k].h[:, :],
                    w_in[128 * k:128 * (k + 1), half * C:(half + 1) * C],
                    [], [winh[k]])
            for m in range(NC_):
                for t in range(NT):
                    ps = PS[2 + (m + t) % 2]
                    for k in range(NF):
                        t_mm(ps.h[:, :], winh[k].h[:, 128 * m:128 * (m + 1)],
                             xlnq[k].h[:, 512 * t:512 * (t + 1)],
                             k == 0, k == NF - 1, [winh[k], xlnq[k]], [ps])
                    scol = q * SQ + 512 * t
                    if half == 0:
                        c = m
                        rx = rawx[(m + t) % 2]
                        v_cp(rx.h[:, 0:3], halo.h[:, 3 * c:3 * (c + 1)],
                             [halo], [rx])
                        s_cp(rx.h[:, 3:515], ps.h[:, :], [ps], [rx])
                        v_cp(halo.h[:, 3 * c:3 * (c + 1)], rx.h[:, 512:515],
                             [rx], [halo])
                        hb = hbuf[(m + t) % 2]
                        v_ts(hb.h[:, :], rx.h[:, 0:512],
                             cw.h[:, 4 * c:4 * c + 1], None, ALU.mult,
                             ALU.bypass, [rx, cw], [hb])
                        for kk in range(1, 4):
                            v_stt(hb.h[:, :], rx.h[:, kk:kk + 512],
                                  cw.h[:, 4 * c + kk:4 * c + kk + 1],
                                  hb.h[:, :], ALU.mult, ALU.add,
                                  [rx, cw, hb], [hb])
                        s_act(hb.h[:, :], hb.h[:, :], AF.Silu, [hb, vec], [hb],
                              bias=CB(c))
                        psm = PS[4 + t % 2]
                        t_mm(psm.h[0:96, :], wx.h[:, 96 * c:96 * (c + 1)],
                             hb.h[:, :], c == 0, c == NC_ - 1, [wx, hb], [psm])
                        if c == NC_ - 1:
                            s_cp(ssm.h[:, scol:scol + 512], psm.h[0:96, :],
                                 [psm], [ssm])
                        dma(ch_sp_h, h_s[128 * c:128 * (c + 1), scol:scol + 512],
                            hb.h[:, :], [hb], [hs_t])
                    else:
                        gb = gbuf[(m + t) % 2]
                        s_cp(gb.h[:, :], ps.h[:, :], [ps], [gb])
                        dma(ch_sp_g,
                            gate_s[128 * m:128 * (m + 1), scol:scol + 512],
                            gb.h[:, :], [gb], [gs_t])

    ch_sp_bc = em.chan("sp_bc")
    bcs_t = T(bc_s, "bc_s")
    dma(ch_sp_bc, bc_s[:, :], ssm.h[64:96, :], [ssm], [bcs_t])

    sb.phase_begin()
    NCQ = 4
    dtq = [AT([128, 512], "dtq%d" % i) for i in range(NCQ)]
    dthq = [AT([128, 512], "dthq%d" % i) for i in range(NCQ)]
    hq = [AT([128, 512], "hq%d" % i) for i in range(NCQ)]
    ch_hq = [em.chan("hq%d" % i) for i in range(NCQ)]
    yacc = [AT([128, 512], "yacc%d" % i) for i in range(NCQ)]
    yq = [AT([128, 512], "yq%d" % c) for c in range(NC_)]
    bb = [AT([128, 512], "bb%d" % i) for i in range(2)]
    cbb = [AT([128, 512], "cbb%d" % i) for i in range(2)]
    ch_bb = [em.chan("bbc%d" % i) for i in range(2)]
    ch_cb = [em.chan("cbc%d" % i) for i in range(2)]
    dA = [AT([128, 512], "dA%d" % i) for i in range(4)]
    dBu = [AT([128, 512], "dBu%d" % i) for i in range(4)]
    stl = [AT([128, 512], "st%d" % i) for i in range(4)]
    gq = [AT([128, 512], "gq%d" % i) for i in range(2)]
    sp1 = AT([128, 512], "sp1")
    sp2 = AT([128, 512], "sp2")
    ch_gq = [em.chan("gq%d" % i) for i in range(2)]
    sgq = AT([128, 512], "sgq")
    woutb = [AT([128, D], "woutb%d" % i) for i in range(2)]
    ch_wout = [em.chan("wout%d" % i) for i in range(2)]
    ast = [AT([128, 512], "ast%d" % k) for k in range(NF)]
    x0r = [AT([128, 512], "x0r%d" % i) for i in range(2)]
    ch_x0r = [em.chan("x0r%d" % i) for i in range(2)]
    ch_sp_x1 = em.chan("sp_x1")
    x1s_t = T(x1_s, "x1_s")

    for q in range(NSQ):
        s0 = q * SQ
        for cq in range(NC_ // NCQ):
            cbase = cq * NCQ
            for ci in range(NCQ):
                c = cbase + ci
                ps = PS[ci % 2]
                t_mm(ps.h[:, :], wdt.h[0:64, 128 * c:128 * (c + 1)],
                     ssm.h[0:64, s0:s0 + 512], True, True, [wdt, ssm], [ps])
                s_act(sp1.h[:, :], ps.h[:, :], AF.Abs, [ps, vec], [sp1],
                      bias=BDT(c))
                s_act(sp2.h[:, :], sp1.h[:, :], AF.Exp, [sp1], [sp2],
                      scale=-1.0)
                s_act(sp2.h[:, :], sp2.h[:, :], AF.Ln, [sp2], [sp2], bias=1.0)
                s_act(sp1.h[:, :], ps.h[:, :], AF.Relu, [ps, vec], [sp1],
                      bias=BDT(c))
                v_tt(ALU.add, dtq[ci].h[:, :], sp1.h[:, :], sp2.h[:, :],
                     [sp1, sp2], [dtq[ci]])
                dma(ch_hq[ci], hq[ci].h[:, :],
                    h_s[128 * c:128 * (c + 1), s0:s0 + 512], [hs_t], [hq[ci]])
                v_tt(ALU.mult, dthq[ci].h[:, :], dtq[ci].h[:, :],
                     hq[ci].h[:, :], [dtq[ci], hq[ci]], [dthq[ci]])
            for j in range(NJ):
                bbt = bb[j % 2]
                cbt = cbb[j % 2]
                dma(ch_bb[j % 2], bbt.h[:, :],
                    bc_s[j:j + 1, s0:s0 + 512].partition_broadcast(128),
                    [bcs_t], [bbt])
                dma(ch_cb[j % 2], cbt.h[:, :],
                    bc_s[16 + j:17 + j, s0:s0 + 512].partition_broadcast(128),
                    [bcs_t], [cbt])
                for ci in range(NCQ):
                    c = cbase + ci
                    cj = 16 * c + j
                    da = dA[ci]
                    db = dBu[ci]
                    st_t = stl[ci]
                    s_act(da.h[:, :], dtq[ci].h[:, :], AF.Exp,
                          [dtq[ci], apb], [da],
                          scale=apb.h[:, cj:cj + 1])
                    v_tt(ALU.mult, db.h[:, :], dthq[ci].h[:, :], bbt.h[:, :],
                         [dthq[ci], bbt], [db])
                    init = 0.0 if q == 0 else carry.h[:, cj:cj + 1]
                    rd = [da, db] + ([carry] if q > 0 else [])
                    v_scan(st_t.h[:, :], da.h[:, :], db.h[:, :], init,
                           rd, [st_t])
                    if q < NSQ - 1:
                        v_cp(carry.h[:, cj:cj + 1], st_t.h[:, 511:512],
                             [st_t], [carry])
                    v_tt(ALU.mult, da.h[:, :], st_t.h[:, :], cbt.h[:, :],
                         [st_t, cbt], [da])
                    if j == 0:
                        v_cp(yacc[ci].h[:, :], da.h[:, :], [da], [yacc[ci]])
                    else:
                        v_tt(ALU.add, yacc[ci].h[:, :], yacc[ci].h[:, :],
                             da.h[:, :], [yacc[ci], da], [yacc[ci]])
            for ci in range(NCQ):
                c = cbase + ci
                gt = gq[ci % 2]
                dma(ch_gq[ci % 2], gt.h[:, :],
                    gate_s[128 * c:128 * (c + 1), s0:s0 + 512], [gs_t], [gt])
                s_act(sgq.h[:, :], gt.h[:, :], AF.Silu, [gt], [sgq])
                v_stt(yacc[ci].h[:, :], hq[ci].h[:, :], DV(c),
                      yacc[ci].h[:, :], ALU.mult, ALU.add,
                      [hq[ci], vec, yacc[ci]], [yacc[ci]])
                v_tt(ALU.mult, yq[c].h[:, :], yacc[ci].h[:, :], sgq.h[:, :],
                     [yacc[ci], sgq], [yq[c]])
        for k in range(NC_):
            wb = woutb[k % 2]
            dma(ch_wout[k % 2], wb.h[:, :], w_out[128 * k:128 * (k + 1), :],
                [], [wb])
            for f in range(NF):
                t_mm(PS[f].h[:, :], wb.h[:, 128 * f:128 * (f + 1)],
                     yq[k].h[:, :], k == 0, k == NC_ - 1, [wb, yq[k]], [PS[f]])
        for f in range(NF):
            s_cp(ast[f].h[:, :], PS[f].h[:, :], [PS[f]], [ast[f]])
        ln_fm([(ast[f], ast[f].h[:, :]) for f in range(NF)],
              lambda k: (ast[k], ast[k].h[:, :]), 512)
        for k in range(NF):
            xr = x0r[k % 2]
            dma(ch_x0r[k % 2], xr.h[:, :],
                x0_s[128 * k:128 * (k + 1), s0:s0 + 512], [x0s_t], [xr])
            v_tt(ALU.add, ast[k].h[:, :], ast[k].h[:, :], xr.h[:, :],
                 [ast[k], xr], [ast[k]])
            dma(ch_sp_x1, x1_s[128 * k:128 * (k + 1), s0:s0 + 512],
                ast[k].h[:, :], [ast[k]], [x1s_t])

    sb.phase_begin()
    masks = AT([128, 4 * 512], "masks")
    rows2 = AT([1, 1152], "rows2")
    xa = [AT([128, S], "xa%d" % k) for k in range(NF)]
    _p3mk = sb.mark()
    x1r = [AT([128, 512], "x1r%d" % k) for k in range(NF)]
    ch_x1r = [em.chan("x1r")] * NF
    gb2 = [AT([128, 512], "gb2_%d" % k) for k in range(NF)]
    ch_sp_g2 = em.chan("sp_g2")
    gs2_t = T(g_s, "g_s")
    ch_masks = em.chan("masks")
    dma(ch_masks, masks.h[:, :], masks_d[:, :], [], [masks])

    for t in range(NQ):
        w0 = 512 * t
        for k in range(NF):
            dma(ch_x1r[k], x1r[k].h[:, :],
                x1_s[128 * k:128 * (k + 1), w0:w0 + 512], [x1s_t], [x1r[k]])
        ln_fm([(x1r[k], x1r[k].h[:, :]) for k in range(NF)],
              lambda k: (gb2[k], gb2[k].h[:, :]), 512)
        for k in range(NF):
            dma(ch_sp_g2, g_s[128 * k:128 * (k + 1), w0:w0 + 512],
                gb2[k].h[:, :], [gb2[k]], [gs2_t])
        ln_fm([(gb2[k], gb2[k].h[:, :]) for k in range(NF)],
              lambda k, w0=w0: (xa[k], xa[k].h[:, w0:w0 + 512]), 512,
              scale=LN1S, shift=LN1B, par_t=vecd)

    sb.reset(_p3mk)
    qhr = [AT([65, S], "qh0")] * 2
    khr = [AT([65, S], "kh0")] * 2
    vh4 = [AT([128, NKT * 65], "vh%d" % i) for i in range(4)]
    wvb = [AT([128, 256], "wvb%d" % k) for k in range(NF)]
    ch_wvb = [em.chan("wvb")] * NF
    wqs = [AT([128, NF * HD], "wqs%d" % i) for i in range(2)]
    wks = [AT([128, NF * HD], "wks%d" % i) for i in range(2)]
    ch_wqs = [em.chan("wqs%d" % i) for i in range(2)]
    ch_wks = [em.chan("wks%d" % i) for i in range(2)]
    pt = [AT([128, 512], "pt%d" % i) for i in range(2)]
    sqh = AT([64, 512], "sqh")
    ctxh = AT([65, S], "ctxh")
    ch_sp_ctx = em.chan("sp_ctx")
    ctxs_t = T(ctx_s, "ctx_s")
    for vt in vh4:
        for kt in range(NKT):
            v_ms(vt.h[:, 65 * kt + 64:65 * kt + 65], 1.0, vt)

    mk_row = rows2.h[0:1, 0:1]
    nrm = rows2.h[0:1, 64:576]
    augr = rows2.h[0:1, 576:1088]
    ch_aug = em.chan("aug")

    ch_spi = [em.chan("spi%d" % i) for i in range(2)]
    ch_sph = [em.chan("sph%d" % i) for i in range(2)]
    ch_splo = [em.chan("splo%d" % i) for i in range(2)]
    w1h_t = T(w1h_s, "w1h_s")
    w1l_t = T(w1l_s, "w1l_s")
    w2h_t = T(w2h_s, "w2h_s")
    w2l_t = T(w2l_s, "w2l_s")
    w3h_t = T(w3h_s, "w3h_s")
    w3l_t = T(w3l_s, "w3l_s")
    spl_jobs = [(w1_r, w1h_t, w1l_t, NE * 16 * 128, 1024),
                (w2_r, w2h_t, w2l_t, NE * 16 * 128, 1024),
                (w3_2d, w3h_t, w3l_t, NE * C, 1024)]
    spl_f = [AT([128, 512], "splf%d" % i) for i in range(2)]
    spl_h = [AT([128, 512], "splh%d" % i, BF16) for i in range(2)]
    spl_l = [AT([128, 512], "spll%d" % i, BF16) for i in range(2)]
    si = 0
    for srcd, hit, lot, nrows, ncols in spl_jobs:
        hid = hit.h
        lod = lot.h
        for r0 in range(0, nrows, 128):
            for c0 in range(0, ncols, 512):
                sl = si % 2
                fx = spl_f[sl]
                gate = [yq[NC_ - 1]] if si == 0 else []
                dma(ch_spi[sl], fx.h[:, :], srcd[r0:r0 + 128, c0:c0 + 512],
                    gate, [fx], eng="gpsimd")
                v_cp(spl_h[sl].h[:, :], fx.h[:, :], [fx], [spl_h[sl]],
                     eng="gpsimd")
                v_tt(ALU.subtract, spl_l[sl].h[:, :], fx.h[:, :],
                     spl_h[sl].h[:, :], [fx, spl_h[sl]], [spl_l[sl]],
                     eng="gpsimd")
                dma(ch_sph[sl], hid[r0:r0 + 128, c0:c0 + 512],
                    spl_h[sl].h[:, :], [spl_h[sl]], [hit], eng="gpsimd")
                dma(ch_splo[sl], lod[r0:r0 + 128, c0:c0 + 512],
                    spl_l[sl].h[:, :], [spl_l[sl]], [lot], eng="gpsimd")
                si += 1

    for hg in range(4):
        for k in range(NF):
            dma(ch_wvb[k], wvb[k].h[:, :],
                wv_d[128 * k:128 * (k + 1), 256 * hg:256 * (hg + 1)],
                [], [wvb[k]])
        for kt in range(NKT):
            ps = PS[kt % 2]
            for k in range(NF):
                t_mm(ps.h[0:128, 0:256], xa[k].h[:, 128 * kt:128 * (kt + 1)],
                     wvb[k].h[:, :], k == 0, k == NF - 1, [xa[k], wvb[k]], [ps])
            for hl in range(4):
                s_cp(vh4[hl].h[:, 65 * kt:65 * kt + 64],
                     ps.h[0:128, 64 * hl:64 * hl + 64], [ps], [vh4[hl]])
      
        for hl in range(4):
            hd_ = 4 * hg + hl
            vh = vh4[hl]
            qh = qhr[hd_ % 2]
            kh = khr[hd_ % 2]
            wq_t = wqs[hd_ % 2]
            wk_t = wks[hd_ % 2]
            dma(ch_wqs[hd_ % 2], wq_t.h[:, :],
                wq_r[128 * hd_:128 * (hd_ + 1), :], [], [wq_t])
            dma(ch_wks[hd_ % 2], wk_t.h[:, :],
                wk_r[128 * hd_:128 * (hd_ + 1), :], [], [wk_t])
            for dst, wt in ((qh, wq_t), (kh, wk_t)):
                for t in range(NQ):
                    ps = PS[2 + t % 2]
                    for k in range(NF):
                        t_mm(ps.h[0:64, :], wt.h[:, 64 * k:64 * (k + 1)],
                             xa[k].h[:, 512 * t:512 * (t + 1)], k == 0,
                             k == NF - 1, [xa[k], wt], [ps])
                    s_cp(dst.h[0:64, 512 * t:512 * (t + 1)], ps.h[0:64, :],
                         [ps], [dst])
            v_ms(kh.h[64:65, :], 1.0, kh)
            for t in range(NQ):
                ps = PS[4]
                s_act(sqh.h[:, :], kh.h[0:64, 512 * t:512 * (t + 1)],
                      AF.Square, [kh], [sqh])
                t_mm(ps.h[0:1, :], ones.h[0:64, 0:1], sqh.h[:, :], True, True,
                     [sqh, ones], [ps])
                s_act(nrm, ps.h[0:1, :], AF.Sqrt, [ps], [rows2])
                if t == 0:
                    v_max(mk_row, nrm, [rows2], [rows2])
                else:
                    v_max(rows2.h[0:1, 1:2], nrm, [rows2], [rows2])
                    v_tt(ALU.max, mk_row, mk_row, rows2.h[0:1, 1:2],
                         [rows2], [rows2])
            for t in range(NQ):
                ps = PS[4]
                s_act(sqh.h[:, :], qh.h[0:64, 512 * t:512 * (t + 1)],
                      AF.Square, [qh], [sqh])
                t_mm(ps.h[0:1, :], ones.h[0:64, 0:1], sqh.h[:, :], True, True,
                     [sqh, ones], [ps])
                s_act(nrm, ps.h[0:1, :], AF.Sqrt, [ps], [rows2])
                v_ts(augr, nrm, mk_row, -1.0, ALU.mult, ALU.mult,
                     [rows2], [rows2])
                dma(ch_aug, qh.h[64:65, 512 * t:512 * (t + 1)], augr,
                    [rows2], [qh])
            for t in range(NQ):
                psc = PS[5 + hd_ % 2] if False else PS[5]
                nkt = 4 * (t + 1)
                for kt in range(nkt):
                    pss = PS[6 + kt % 2]
                    t_mm(pss.h[:, :], kh.h[0:65, 128 * kt:128 * (kt + 1)],
                         qh.h[0:65, 512 * t:512 * (t + 1)], True, True,
                         [kh, qh], [pss])
                    pb = pt[kt % 2]
                    s_act(pb.h[:, :], pss.h[:, :], AF.Exp, [pss], [pb],
                          scale=0.125)
                    r = kt - 4 * t
                    if r >= 0:
                        v_tt(ALU.mult, pb.h[:, :], pb.h[:, :],
                             masks.h[:, 512 * r:512 * (r + 1)],
                             [pb, masks], [pb])
                    t_mm(psc.h[0:65, :], vh.h[:, 65 * kt:65 * (kt + 1)],
                         pb.h[:, :], kt == 0, kt == nkt - 1, [vh, pb], [psc])
                s_cp(ctxh.h[0:65, 512 * t:512 * (t + 1)], psc.h[0:65, :],
                     [psc], [ctxh])
                v_rec(ctxh.h[64:65, 512 * t:512 * (t + 1)],
                      ctxh.h[64:65, 512 * t:512 * (t + 1)], [ctxh], [ctxh])
                psr = PS[4]
                t_mm(psr.h[0:64, :], ones.h[64:65, 0:64],
                     ctxh.h[64:65, 512 * t:512 * (t + 1)], True, True,
                     [ctxh, ones], [psr])
                v_tt(ALU.mult, ctxh.h[0:64, 512 * t:512 * (t + 1)],
                     ctxh.h[0:64, 512 * t:512 * (t + 1)], psr.h[0:64, :],
                     [ctxh, psr], [ctxh])
            dma(ch_sp_ctx, ctx_s[64 * hd_:64 * (hd_ + 1), :], ctxh.h[0:64, :],
                [ctxh], [ctxs_t])

    sb.phase_begin()
    wob = [AT([128, D], "wob%d" % i) for i in range(2)]
    ch_wob = [em.chan("wob%d" % i) for i in range(2)]
    ctxr = [AT([128, 512], "ctxr%d" % i) for i in range(3)]
    ch_ctxr = [em.chan("ctxr%d" % i) for i in range(3)]
    gr3 = [AT([128, 512], "gr3_%d" % k) for k in range(NF)]
    ch_gr3 = [em.chan("gr3")] * NF
    ab = [AT([128, 512], "ab%d" % k) for k in range(NF)]
    xl2 = [AT([128, 512], "xl2_%d" % k) for k in range(NF)]
    xl2ht = [AT([128, 512], "xl2ht%d" % i, BF16) for i in range(2)]
    xl2lt = [AT([128, 512], "xl2lt%d" % i, BF16) for i in range(2)]
    ch_sp_a = em.chan("sp_a")
    ch_sp_xl2 = em.chan("sp_xl2")
    ch_sp_xl2l = em.chan("sp_xl2l")
    as_t = T(a_s, "a_s")
    xl2hs_t = T(xl2h_s, "xl2h_s")
    xl2ls_t = T(xl2l_s, "xl2l_s")

    for t in range(NQ):
        w0 = 512 * t
        for k in range(NF):
            cr = ctxr[k % 3]
            dma(ch_ctxr[k % 3], cr.h[:, :],
                ctx_s[128 * k:128 * (k + 1), w0:w0 + 512], [ctxs_t], [cr])
            wb = wob[k % 2]
            dma(ch_wob[k % 2], wb.h[:, :], wo[128 * k:128 * (k + 1), :],
                [], [wb])
            for f in range(NF):
                t_mm(PS[f].h[:, :], wb.h[:, 128 * f:128 * (f + 1)],
                     cr.h[:, :], k == 0, k == NF - 1, [wb, cr], [PS[f]])
        for f in range(NF):
            dma(ch_gr3[f], gr3[f].h[:, :],
                g_s[128 * f:128 * (f + 1), w0:w0 + 512], [gs2_t], [gr3[f]])
        for f in range(NF):
            v_stt(ab[f].h[:, :], PS[f].h[:, :], BO(f), gr3[f].h[:, :],
                  ALU.add, ALU.add, [PS[f], vecd, gr3[f]], [ab[f]])
            dma(ch_sp_a, a_s[128 * f:128 * (f + 1), w0:w0 + 512],
                ab[f].h[:, :], [ab[f]], [as_t])
        ln_fm([(ab[f], ab[f].h[:, :]) for f in range(NF)],
              lambda k: (xl2[k], xl2[k].h[:, :]), 512,
              scale=LN2S, shift=LN2B, par_t=vecd)
        for k in range(NF):
            ht = xl2ht[k % 2]
            lt = xl2lt[k % 2]
            s_cp(ht.h[:, :], xl2[k].h[:, :], [xl2[k]], [ht])
            v_tt(ALU.subtract, lt.h[:, :], xl2[k].h[:, :], ht.h[:, :],
                 [xl2[k], ht], [lt])
            dma(ch_sp_xl2, xl2h_s[128 * k:128 * (k + 1), w0:w0 + 512],
                ht.h[:, :], [ht], [xl2hs_t])
            dma(ch_sp_xl2l, xl2l_s[128 * k:128 * (k + 1), w0:w0 + 512],
                lt.h[:, :], [lt], [xl2ls_t])

    sb.phase_begin()
    xhh = [AT([128, SH], "xhh%d" % k, BF16) for k in range(NF)]
    xll = [AT([128, SH], "xll%d" % k, BF16) for k in range(NF)]
    ch_xhh = [em.chan("xlh")] * NF
    ch_xll = [em.chan("xlh")] * NF
    heh = [AT([128, SH], "heh%d" % m, BF16) for m in range(8)]
    hel = [AT([128, SH], "hel%d" % m, BF16) for m in range(8)]
    hf = [AT([128, 512], "hf0")] * 2
    acc = [AT([128, SH], "acc%d" % k) for k in range(NF)]
    w1b = [AT([128, 8 * 128], "w1b%d" % i, BF16) for i in range(4)]
    w2b = [AT([128, 8 * 128], "w2b%d" % i, BF16) for i in range(4)]
    _w12a, _w12b = em.chan("w12a"), em.chan("w12b")
    ch_w1b = [_w12a, _w12a, _w12b, _w12b]
    ch_w2b = [_w12a, _w12a, _w12b, _w12b]
    w3hb = [AT([128, 512], "w3hb%d" % i, BF16) for i in range(3)]
    w3lb = [AT([128, 512], "w3lb%d" % i, BF16) for i in range(3)]
    ch_w3 = [em.chan("w3c%d" % i) for i in range(3)]
    silu_b = AT([128, 512], "silu_b")
    tmpb = AT([128, 512], "tmpb")
    wtm = AT([128, 64], "wtm")
    wte = AT([8, SH], "wte")
    wtes_t = T(wte_s, "wte_s")
    ch_sp_wte = em.chan("sp_wte")
    routeb = AT([128, SH], "routeb")
    ch_rb = em.chan("rb")
    gsc = AT([8, 512], "gsc")
    r8 = AT([128, 8], "r8")
    r8b = AT([128, 8], "r8b")
    rcol = AT([128, 4], "rcol")
    ar2 = [AT([128, 512], "ar2_%d" % k) for k in range(NF)]
    ch_ar2 = [em.chan("ar2")] * NF
    x1r2 = [AT([128, 512], "x1r2_%d" % i) for i in range(2)]
    ch_x1r2 = [em.chan("x1r2%d" % i) for i in range(2)]
    outst = AT([128, 512], "outst")
    ch_out = em.chan("outc")
    out_t = T(out, "out")

    NTH = SH // 512
    for sh in range(2):
        s0 = sh * SH
        for k in range(NF):
            dma(ch_xhh[k], xhh[k].h[:, :],
                xl2h_s[128 * k:128 * (k + 1), s0:s0 + SH], [xl2hs_t], [xhh[k]])
            dma(ch_xll[k], xll[k].h[:, :],
                xl2l_s[128 * k:128 * (k + 1), s0:s0 + SH], [xl2ls_t], [xll[k]])
            v_ms(acc[k].h[:, :], 0.0, acc[k])
        for t in range(NTH):
            ps = PS[0]
            for k in range(NF):
                mm3(ps.h[0:8, :], gwh.h[:, 8 * k:8 * (k + 1)],
                    gwl.h[:, 8 * k:8 * (k + 1)],
                    xhh[k].h[:, 512 * t:512 * (t + 1)],
                    xll[k].h[:, 512 * t:512 * (t + 1)], k == 0, k == NF - 1,
                    [gwh, gwl, xhh[k], xll[k]], [ps])
            s_cp(gsc.h[0:8, :], ps.h[0:8, :], [ps], [gsc])
            for si in range(4):
                sc = 4 * t + si
                ps2 = PS[1]
                t_tr(ps2.h[0:128, 0:8], gsc.h[0:8, 128 * si:128 * (si + 1)],
                     [gsc], [ps2], idw=8)
                s_cp(r8.h[:, :], ps2.h[0:128, 0:8], [ps2], [r8])
                v_max(rcol.h[:, 0:1], r8.h[:, :], [r8], [rcol])
                v_ts(r8b.h[:, :], r8.h[:, :], rcol.h[:, 0:1], -1e30,
                     ALU.is_equal, ALU.mult, [r8, rcol], [r8b])
                v_tt(ALU.add, r8b.h[:, :], r8.h[:, :], r8b.h[:, :],
                     [r8, r8b], [r8b])
                v_max(rcol.h[:, 1:2], r8b.h[:, :], [r8b], [rcol])
                v_tt(ALU.subtract, rcol.h[:, 2:3], rcol.h[:, 1:2],
                     rcol.h[:, 0:1], [rcol], [rcol])
                s_act(rcol.h[:, 2:3], rcol.h[:, 2:3], AF.Exp, [rcol], [rcol])
                v_ts(rcol.h[:, 3:4], rcol.h[:, 2:3], 1.0, None, ALU.add,
                     ALU.bypass, [rcol], [rcol])
                v_rec(rcol.h[:, 3:4], rcol.h[:, 3:4], [rcol], [rcol])
                v_tt(ALU.mult, rcol.h[:, 2:3], rcol.h[:, 2:3],
                     rcol.h[:, 3:4], [rcol], [rcol])
                v_ts(wtm.h[:, 0:8], r8.h[:, :], rcol.h[:, 0:1],
                     rcol.h[:, 3:4], ALU.is_equal, ALU.mult,
                     [r8, rcol], [wtm])
                v_ts(r8b.h[:, :], r8b.h[:, :], rcol.h[:, 1:2],
                     rcol.h[:, 2:3], ALU.is_equal, ALU.mult,
                     [r8b, rcol], [r8b])
                v_tt(ALU.add, wtm.h[:, 0:8], wtm.h[:, 0:8], r8b.h[:, :],
                     [wtm, r8b], [wtm])
                ps3 = PS[2]
                t_tr(ps3.h[0:8, 0:128], wtm.h[:, 0:8], [wtm], [ps3])
                s_cp(wte.h[0:8, 128 * sc:128 * (sc + 1)], ps3.h[0:8, 0:128],
                     [ps3], [wte])
        dma(ch_sp_wte, wte_s[0:8, s0:s0 + SH], wte.h[0:8, :], [wte], [wtes_t])
        for e in range(NE):
            dma(ch_rb, routeb.h[:, :],
                wte_s[e:e + 1, s0:s0 + SH].partition_broadcast(128),
                [wtes_t], [routeb])
            for mh in range(2):
                for mi in range(8):
                    m = 8 * mh + mi
                    b1h = w1b[2 * (mi % 2)]
                    b1l = w1b[2 * (mi % 2) + 1]
                    b2h = w2b[2 * (mi % 2)]
                    b2l = w2b[2 * (mi % 2) + 1]
                    row0 = (e * 16 + m) * 128
                    dma(ch_w1b[2 * (mi % 2)], b1h.h[:, :],
                        w1h_s[row0:row0 + 128, :], [w1h_t], [b1h])
                    dma(ch_w1b[2 * (mi % 2) + 1], b1l.h[:, :],
                        w1l_s[row0:row0 + 128, :], [w1l_t], [b1l])
                    dma(ch_w2b[2 * (mi % 2)], b2h.h[:, :],
                        w2h_s[row0:row0 + 128, :], [w2h_t], [b2h])
                    dma(ch_w2b[2 * (mi % 2) + 1], b2l.h[:, :],
                        w2l_s[row0:row0 + 128, :], [w2l_t], [b2l])
                    for t in range(NTH):
                        p1 = PS[0 + t % 2]
                        p2 = PS[2 + t % 2]
                        for k in range(NF):
                            mm3(p1.h[:, :], b1h.h[:, 128 * k:128 * (k + 1)],
                                b1l.h[:, 128 * k:128 * (k + 1)],
                                xhh[k].h[:, 512 * t:512 * (t + 1)],
                                xll[k].h[:, 512 * t:512 * (t + 1)],
                                k == 0, k == NF - 1,
                                [b1h, b1l, xhh[k], xll[k]], [p1])
                        for k in range(NF):
                            mm3(p2.h[:, :], b2h.h[:, 128 * k:128 * (k + 1)],
                                b2l.h[:, 128 * k:128 * (k + 1)],
                                xhh[k].h[:, 512 * t:512 * (t + 1)],
                                xll[k].h[:, 512 * t:512 * (t + 1)],
                                k == 0, k == NF - 1,
                                [b2h, b2l, xhh[k], xll[k]], [p2])
                        s_act(silu_b.h[:, :], p1.h[:, :], AF.Silu,
                              [p1], [silu_b])
                        hft = hf[t % 2]
                        v_tt(ALU.mult, hft.h[:, :],
                             silu_b.h[:, :], p2.h[:, :], [silu_b, p2], [hft])
                        s_cp(heh[mi].h[:, 512 * t:512 * (t + 1)], hft.h[:, :],
                             [hft], [heh[mi]])
                        v_tt(ALU.subtract,
                             hel[mi].h[:, 512 * t:512 * (t + 1)], hft.h[:, :],
                             heh[mi].h[:, 512 * t:512 * (t + 1)],
                             [hft, heh[mi]], [hel[mi]])
                for fh in range(2):
                    for ki in range(8):
                        k = 8 * mh + ki
                        wbh = w3hb[ki % 3]
                        wbl = w3lb[ki % 3]
                        dma(ch_w3[ki % 3], wbh.h[:, :],
                            w3h_s[e * C + 128 * k:e * C + 128 * (k + 1),
                                  512 * fh:512 * (fh + 1)], [w3h_t], [wbh])
                        dma(ch_w3[ki % 3], wbl.h[:, :],
                            w3l_s[e * C + 128 * k:e * C + 128 * (k + 1),
                                  512 * fh:512 * (fh + 1)], [w3l_t], [wbl])
                        for t in range(NTH):
                            for f in range(4):
                                mm3(PS[4 * t + f].h[:, :],
                                    wbh.h[:, 128 * f:128 * (f + 1)],
                                    wbl.h[:, 128 * f:128 * (f + 1)],
                                    heh[ki].h[:, 512 * t:512 * (t + 1)],
                                    hel[ki].h[:, 512 * t:512 * (t + 1)],
                                    ki == 0, ki == 7,
                                    [wbh, wbl, heh[ki], hel[ki]],
                                    [PS[4 * t + f]])
                    for t in range(NTH):
                        for f in range(4):
                            fi = 4 * fh + f
                            v_tt(ALU.mult, tmpb.h[:, :],
                                 PS[4 * t + f].h[:, :],
                                 routeb.h[:, 512 * t:512 * (t + 1)],
                                 [PS[4 * t + f], routeb], [tmpb])
                            v_tt(ALU.add,
                                 acc[fi].h[:, 512 * t:512 * (t + 1)],
                                 acc[fi].h[:, 512 * t:512 * (t + 1)],
                                 tmpb.h[:, :], [acc[fi], tmpb], [acc[fi]])
        for t in range(NTH):
            w0 = s0 + 512 * t
            for k in range(NF):
                dma(ch_ar2[k], ar2[k].h[:, :],
                    a_s[128 * k:128 * (k + 1), w0:w0 + 512], [as_t], [ar2[k]])
            for k in range(NF):
                v_tt(ALU.add, ar2[k].h[:, :], ar2[k].h[:, :],
                     acc[k].h[:, 512 * t:512 * (t + 1)],
                     [ar2[k], acc[k]], [ar2[k]])
            ln_fm([(ar2[k], ar2[k].h[:, :]) for k in range(NF)],
                  lambda k: (ar2[k], ar2[k].h[:, :]), 512)
            for k in range(NF):
                xr = x1r2[k % 2]
                dma(ch_x1r2[k % 2], xr.h[:, :],
                    x1_s[128 * k:128 * (k + 1), w0:w0 + 512], [x1s_t], [xr])
                v_tt(ALU.add, ar2[k].h[:, :], ar2[k].h[:, :], xr.h[:, :],
                     [ar2[k], xr], [ar2[k]])
            for sc in range(4):
                for kh2 in range(2):
                    for kk in range(4):
                        k = 4 * kh2 + kk
                        ps = PS[k % 2]
                        t_tr(ps.h[:, 0:128],
                             ar2[k].h[:, 128 * sc:128 * (sc + 1)],
                             [ar2[k]], [ps])
                        s_cp(outst.h[:, 128 * kk:128 * (kk + 1)],
                             ps.h[:, 0:128], [ps], [outst])
                    dma(ch_out,
                        out[w0 + 128 * sc:w0 + 128 * (sc + 1),
                            512 * kh2:512 * (kh2 + 1)],
                        outst.h[:, :], [outst], [out_t])

    em.emit("sync", None, reads=[out_t], writes=[])
    em.finalize()
    es.close()
    return nc


def _prep(inputs):
    ct = np.ascontiguousarray
    f32 = lambda a: np.asarray(a, dtype=np.float32)

    def head_repack(w):
        return ct(w.reshape(NF, 128, NH, HD).transpose(2, 1, 0, 3)
                  .reshape(NH * 128, NF * HD))

    def w12_repack(w):
        return ct(w.reshape(NE, NF, 128, NC_, 128).transpose(0, 3, 2, 1, 4)
                  .reshape(NE * NC_ * 128, NF * 128))

    def vchunks(v, n):
        return ct(v.reshape(n, 128).T)

    conv_w = f32(inputs["L0_attn_conv_w"])
    vecs = np.zeros((128, 48), np.float32)
    vecs[:, 0:16] = vchunks(f32(inputs["L0_attn_conv_b"]), 16)
    vecs[:, 16:32] = vchunks(f32(inputs["L0_attn_b_dt"]), 16)
    vecs[:, 32:48] = vchunks(f32(inputs["L0_attn_D"]), 16)
    vecd = np.zeros((128, 40), np.float32)
    vecd[:, 0:8] = vchunks(f32(inputs["L0_ffn_ln1_scale"]), 8)
    vecd[:, 8:16] = vchunks(f32(inputs["L0_ffn_ln1_shift"]), 8)
    vecd[:, 16:24] = vchunks(f32(inputs["L0_ffn_ln2_scale"]), 8)
    vecd[:, 24:32] = vchunks(f32(inputs["L0_ffn_ln2_shift"]), 8)
    vecd[:, 32:40] = vchunks(f32(inputs["L0_ffn_bo"]), 8)
    cw_r = ct(conv_w[:, 0, :].T.reshape(16, 128, 4).transpose(1, 0, 2)
              .reshape(128, 64))
    alog_r = ct(f32(inputs["L0_attn_A_log"]).reshape(16, 128, 16)
                .transpose(1, 0, 2).reshape(128, 256))
    gw_r = ct(f32(inputs["L0_ffn_gate_w"]).reshape(8, 128, 8)
              .transpose(1, 0, 2).reshape(128, 64))
    masks = np.zeros((128, 4 * 512), np.float32)
    i = np.arange(128)[:, None]
    j = np.arange(512)[None, :]
    for r in range(4):
        masks[:, 512 * r:512 * (r + 1)] = (j - 128 * r >= i)

    return {
        "w_in": f32(inputs["L0_attn_W_in"]),
        "w_x": f32(inputs["L0_attn_W_x"]),
        "w_dt": f32(inputs["L0_attn_W_dt"]),
        "w_out": f32(inputs["L0_attn_W_out"]),
        "wq_r": head_repack(f32(inputs["L0_ffn_wq"])),
        "wk_r": head_repack(f32(inputs["L0_ffn_wk"])),
        "wv": f32(inputs["L0_ffn_wv"]),
        "wo": f32(inputs["L0_ffn_wo"]),
        "w1_r": w12_repack(f32(inputs["L0_ffn_w1"])),
        "w2_r": w12_repack(f32(inputs["L0_ffn_w2"])),
        "w3": ct(f32(inputs["L0_ffn_w3"]).reshape(NE * C, D)),
        "vecs": vecs, "vecd": vecd, "cw_r": cw_r, "alog_r": alog_r,
        "gw_r": gw_r, "ident": np.eye(128, dtype=np.float32), "masks": masks,
    }


def kernel(**inputs):
    S = 2048
    x = np.asarray(inputs["L0_attn_x"], dtype=np.float32)
    B = x.shape[0]
    shared = _prep(inputs)
    nc = build_program(S)
    in_maps = [dict(shared, x=np.ascontiguousarray(x[i])) for i in range(B)]
    res = run_bass_kernel_spmd(nc, in_maps, core_ids=list(range(B)))
    return np.stack([np.asarray(r["out"]) for r in res.results], axis=0)
